# version 01
# baseline (speedup 1.0000x reference)
import math
from contextlib import ExitStack

import numpy as np
import ml_dtypes
import concourse.bass as bass
import concourse.mybir as mybir
from concourse.bass_utils import run_bass_kernel_spmd

F32 = mybir.dt.float32
BF16 = mybir.dt.bfloat16
I32 = mybir.dt.int32
AF = mybir.ActivationFunctionType
ALU = mybir.AluOpType
AX = mybir.AxisListType.X

D = 1024
NH = 16
DH = 64
SEQ = 4096
TQ = 2048
NCORES = 8
LN_EPS = 1e-5
TWO_PI = 2.0 * math.pi
RR_C1 = 6.28125
RR_C2 = TWO_PI - RR_C1

bfd = ml_dtypes.bfloat16



class Scope:
    def __init__(self, prog, ctx):
        self.p = prog
        self.ctx = ctx

    def sb(self, name, shape, dt):
        return self.ctx.enter_context(self.p.nc.sbuf_tensor(name, shape, dt))

    def ps(self, name, shape, dt=F32):
        return self.ctx.enter_context(self.p.nc.psum_tensor(name, shape, dt))


class Prog:
    def __init__(self):
        self.nc = bass.Bass("TRN2", target_bir_lowering=False)
        self.ctx = ExitStack()
        self._sems = {}
        self._uid = 0

    def uid(self, base):
        self._uid += 1
        return "%s_%d" % (base, self._uid)

    def sem(self, name):
        if name not in self._sems:
            h = self.ctx.enter_context(self.nc.semaphore(name))
            self._sems[name] = [h, 0]
        return self._sems[name][0]

    def bump(self, name, by=1):
        s = self._sems[name]
        s[1] += by
        return s[1]

    def count(self, name):
        self.sem(name)
        return self._sems[name][1]

    def sb(self, name, shape, dt):
        return self.ctx.enter_context(self.nc.sbuf_tensor(name, shape, dt))

    def scoped(self):
        return _ScopeCtx(self)

    def dram(self, name, shape, dt):
        return self.nc.dram_tensor(name, shape, dt)

    def q(self, eng, inc, fn, wait=True):
        s = self.sem("q")
        if wait:
            eng.wait_ge(s, self.count("q"))
        ins = fn()
        if inc:
            ins.then_inc(s, inc)
            self.bump("q", inc)
        return ins

    def finish(self):
        self.nc.gpsimd.wait_ge(self.sem("q"), self.count("q"))

    def close(self):
        self.ctx.close()


class _ScopeCtx:
    def __init__(self, prog):
        self.p = prog

    def __enter__(self):
        self._es = ExitStack()
        self._es.__enter__()
        return Scope(self.p, self._es)

    def __exit__(self, *a):
        return self._es.__exit__(*a)


def hT(w):
    return np.ascontiguousarray(np.asarray(w, dtype=np.float32).T)


def setup_consts(p):
    nc = p.nc
    c = {
        "ident": p.sb("c_ident", [128, 128], BF16),
        "ones_col_bf": p.sb("c_ones_col_bf", [128, 1], BF16),
        "ones_row_f": p.sb("c_ones_row_f", [1, 512], F32),
        "eps_col": p.sb("c_eps_col", [128, 1], F32),
        "ident_f": p.sb("c_ident_f", [32, 32], F32),
    }
    g = nc.gpsimd
    p.q(g, 0, lambda: g.memset(c["ident"][:, :], 0.0), wait=False)
    p.q(g, 1, lambda: g.affine_select(
        out=c["ident"][:, :], in_=c["ident"][:, :], compare_op=ALU.not_equal,
        fill=1.0, base=0, pattern=[[-1, 128]], channel_multiplier=1), wait=False)
    p.q(g, 1, lambda: g.memset(c["ones_col_bf"][:, :], 1.0))
    p.q(g, 1, lambda: g.memset(c["ones_row_f"][:, :], 1.0))
    p.q(g, 1, lambda: g.memset(c["eps_col"][:, :], LN_EPS))
    p.q(g, 0, lambda: g.memset(c["ident_f"][:, :], 0.0), wait=False)
    p.q(g, 1, lambda: g.affine_select(
        out=c["ident_f"][:, :], in_=c["ident_f"][:, :], compare_op=ALU.not_equal,
        fill=1.0, base=0, pattern=[[-1, 32]], channel_multiplier=1), wait=False)
    return c



def emit_ln_transpose(p, c, x_d, x_row0, T, xhatT_d, gb=None,
                      tokmajor_d=None, f32_d=None):
    nc = p.nc
    v, s, sy, t = nc.vector, nc.scalar, nc.sync, nc.tensor
    with p.scoped() as sc:
        name = p.uid("ln")
        xa = sc.sb(name + "_x", [128, D], F32)
        sq = sc.sb(name + "_sq", [128, D], F32)
        s1 = sc.sb(name + "_s1", [128, 1], F32)
        s2 = sc.sb(name + "_s2", [128, 1], F32)
        nmean = sc.sb(name + "_nm", [128, 1], F32)
        tvar = sc.sb(name + "_tv", [128, 1], F32)
        stdb = sc.sb(name + "_std", [128, 1], F32)
        rstd = sc.sb(name + "_rs", [128, 1], F32)
        xhf = sc.sb(name + "_xhf", [128, D], F32)
        xh = sc.sb(name + "_xh", [128, D], BF16)
        xT = sc.sb(name + "_xT", [128, D], BF16)
        ptr = sc.ps(name + "_ptr", [128, 128], BF16)
        for i in range(T // 128):
            r0 = x_row0 + i * 128
            p.q(sy, 16, lambda r0=r0: sy.dma_start(xa[:, :], x_d[r0:r0 + 128, :]))
            p.q(s, 1, lambda: s.activation(sq[:, :], xa[:, :], AF.Square,
                                           accum_out=s2[:, 0:1]))
            p.q(v, 1, lambda: v.reduce_sum(s1[:, :], xa[:, :], axis=AX))
            p.q(v, 1, lambda: v.tensor_scalar(nmean[:, :], s1[:, :], -1.0 / D,
                                              None, ALU.mult))
            p.q(v, 1, lambda: v.tensor_scalar(tvar[:, :], s2[:, :], 1.0 / D,
                                              None, ALU.mult))
            p.q(v, 1, lambda: v.scalar_tensor_tensor(
                tvar[:, :], nmean[:, :], nmean[:, 0:1], tvar[:, :],
                ALU.mult, ALU.subtract))
            p.q(s, 1, lambda: s.activation(stdb[:, :], tvar[:, :], AF.Sqrt,
                                           bias=c["eps_col"][:, 0:1], scale=-1.0))
            p.q(v, 1, lambda: v.reciprocal(rstd[:, :], stdb[:, :]))
            p.q(v, 1, lambda: v.tensor_scalar(xhf[:, :], xa[:, :], nmean[:, 0:1],
                                              rstd[:, 0:1], ALU.add, ALU.mult))
            if f32_d is not None:
                p.q(sy, 16, lambda i=i: sy.dma_start(
                    f32_d[i * 128:(i + 1) * 128, :], xhf[:, :]))
            if gb is not None:
                gam, bet = gb
                p.q(v, 1, lambda: v.tensor_mul(xhf[:, :], xhf[:, :], gam[:, :]))
                p.q(v, 1, lambda: v.tensor_add(xh[:, :], xhf[:, :], bet[:, :]))
            else:
                p.q(v, 1, lambda: v.tensor_copy(xh[:, :], xhf[:, :]))
            if tokmajor_d is not None:
                p.q(sy, 16, lambda i=i: sy.dma_start(
                    tokmajor_d[i * 128:(i + 1) * 128, :], xh[:, :]))
            for dc in range(8):
                p.q(t, 1, lambda dc=dc: t.transpose(
                    ptr[:, :], xh[:, dc * 128:(dc + 1) * 128], c["ident"][:, :]))
                p.q(v, 1, lambda dc=dc: v.tensor_copy(
                    xT[:, dc * 128:(dc + 1) * 128], ptr[:, :]))
            p.q(sy, 16, lambda i=i: sy.dma_start(
                xhatT_d.ap()[:, :, i * 128:(i + 1) * 128],
                xT.rearrange("p (dc n) -> p dc n", dc=8)))



def load_weight_ws(p, sc, w_ap, din, dout, name=None):
    nc = p.nc
    name = name or p.uid("w")
    w_sb = sc.sb(name, [128, (din // 128) * dout], BF16)
    src = w_ap.rearrange("(dc prt) o -> prt dc o", prt=128)
    p.q(nc.gpsimd, 16, lambda: nc.gpsimd.dma_start(
        w_sb.rearrange("p (dc o) -> p dc o", dc=din // 128), src))
    return w_sb


def load_bias_cols(p, sc, b_ap, dout, name=None):
    nc = p.nc
    b_sb = sc.sb(name or p.uid("b"), [128, dout // 128], F32)
    p.q(nc.sync, 16, lambda: nc.sync.dma_start(b_sb[:, :], b_ap))
    return b_sb


def hBC(b):
    return np.ascontiguousarray(np.asarray(b, dtype=np.float32).reshape(-1, 128).T)


def emit_bias_replicate(p, c, sc, b_ap, dout=D, name=None):
    nc = p.nc
    name = name or p.uid("brep")
    row = sc.sb(name + "_row", [1, dout], F32)
    rep = sc.sb(name + "_rep", [128, dout], F32)
    p.q(nc.sync, 16, lambda: nc.sync.dma_start(row[:, :], b_ap[None, :]))
    with p.scoped() as sc2:
        pb = sc2.ps(name + "_ps", [128, 512], F32)
        for j in range(dout // 512):
            p.q(nc.tensor, 1, lambda j=j: nc.tensor.matmul(
                pb[:, :], c["ones_row_f"][0:1, 0:128],
                row[0:1, j * 512:(j + 1) * 512], start=True, stop=True))
            p.q(nc.vector, 1, lambda j=j: nc.vector.tensor_copy(
                rep[:, j * 512:(j + 1) * 512], pb[:, :]))
    return rep



def emit_proj_ws(p, w_sb, din, dout, xhatT_d, T, out_dT, col0=0, w_dout=None,
                 w_col0=0, bias_cols=None, bias_off=0, act=None, scale=None,
                 out_pair_off=0, xhat_ndc=None):
    nc = p.nc
    v, s, sy, t = nc.vector, nc.scalar, nc.sync, nc.tensor
    ndc = din // 128
    w_dout = w_dout if w_dout is not None else dout
    with p.scoped() as sc:
        name = p.uid("pws")
        xin = sc.sb(name + "_xin", [128, ndc * 512], BF16)
        ot = sc.sb(name + "_o", [128, 512], BF16)
        pp = sc.ps(name + "_ps", [128, 512], F32)
        for n in range(T // 512):
            src_cols = slice(col0 + n * 512, col0 + (n + 1) * 512)
            p.q(sy, 16, lambda src_cols=src_cols: sy.dma_start(
                xin.rearrange("p (dc n) -> p dc n", dc=ndc),
                xhatT_d.ap()[:, 0:ndc, src_cols]))
            for do in range(dout // 128):
                for dc in range(ndc):
                    col = dc * w_dout + w_col0 + do * 128
                    p.q(t, 1 if dc == ndc - 1 else 0,
                        lambda col=col, dc=dc: t.matmul(
                            pp[:, :], w_sb[:, col:col + 128],
                            xin[:, dc * 512:(dc + 1) * 512],
                            start=(dc == 0), stop=(dc == ndc - 1)),
                        wait=(dc == 0))
                bias_ap = (bias_cols[:, bias_off + do:bias_off + do + 1]
                           if bias_cols is not None else None)
                if act is not None:
                    p.q(s, 1, lambda bias_ap=bias_ap: s.activation(
                        ot[:, :], pp[:, :], act,
                        bias=bias_ap if bias_ap is not None else 0.0))
                elif scale is not None:
                    b = bias_ap if bias_ap is not None else 0.0
                    p.q(v, 1, lambda b=b: v.tensor_scalar(
                        ot[:, :], pp[:, :], b, scale, ALU.add, ALU.mult))
                elif bias_ap is not None:
                    p.q(v, 1, lambda bias_ap=bias_ap: v.tensor_scalar(
                        ot[:, :], pp[:, :], bias_ap, None, ALU.add))
                else:
                    p.q(v, 1, lambda: v.tensor_copy(ot[:, :], pp[:, :]))
                p.q(sy, 16, lambda do=do, n=n: sy.dma_start(
                    out_dT.ap()[:, out_pair_off + do, n * 512:(n + 1) * 512],
                    ot[:, :]))


def emit_proj_xs(p, w_sb, din, dout, actT_d, T, evac, pair_off=0, w_dout=None,
                 w_col0=0):
    nc = p.nc
    sy, t = nc.sync, nc.tensor
    ndc = din // 128
    w_dout = w_dout if w_dout is not None else dout
    og_w = min(512, dout)
    with p.scoped() as sc:
        name = p.uid("pxs")
        xin = sc.sb(name + "_xin", [128, ndc * 128], BF16)
        pp = sc.ps(name + "_ps", [128, og_w], F32)
        for ti in range(T // 128):
            p.q(sy, 16, lambda ti=ti: sy.dma_start(
                xin.rearrange("p (dc n) -> p dc n", dc=ndc),
                actT_d.ap()[:, pair_off:pair_off + ndc, ti * 128:(ti + 1) * 128]))
            for og in range(dout // og_w):
                for dc in range(ndc):
                    col = dc * w_dout + w_col0 + og * og_w
                    p.q(t, 1 if dc == ndc - 1 else 0,
                        lambda col=col, dc=dc: t.matmul(
                            pp[:, :], xin[:, dc * 128:(dc + 1) * 128],
                            w_sb[:, col:col + og_w],
                            start=(dc == 0), stop=(dc == ndc - 1)),
                        wait=(dc == 0))
                evac(ti, og, pp)



def emit_norm2_max(p, c, featT_d, pair_slots, T, out11):
    nc = p.nc
    v, s, sy, t = nc.vector, nc.scalar, nc.sync, nc.tensor
    with p.scoped() as sc:
        name = p.uid("nrm")
        kin = sc.sb(name + "_kin", [128, 512], BF16)
        ksq = sc.sb(name + "_ksq", [128, 512], BF16)
        t11 = sc.sb(name + "_t11", [1, 1], F32)
        pn = sc.ps(name + "_ps", [1, 512], F32)
        p.q(v, 1, lambda: v.memset(out11[:, :], 0.0))
        for pr in pair_slots:
            for n in range(T // 512):
                p.q(sy, 16, lambda pr=pr, n=n: sy.dma_start(
                    kin[:, :], featT_d.ap()[:, pr, n * 512:(n + 1) * 512]))
                p.q(s, 1, lambda: s.activation(ksq[:, :], kin[:, :], AF.Square))
                for half in (0, 64):
                    p.q(t, 1, lambda half=half: t.matmul(
                        pn[:, :], c["ones_col_bf"][half:half + 64, 0:1],
                        ksq[half:half + 64, :], start=True, stop=True))
                    p.q(v, 1, lambda: v.reduce_max(t11[:, :], pn[0:1, :], axis=AX))
                    p.q(v, 1, lambda: v.tensor_tensor(
                        out11[:, :], out11[:, :], t11[:, :], ALU.max))


def emit_mhat_row(p, c, qmax2, kmax2, mh_row, T, extra11=None, take_sqrt=True):
    nc = p.nc
    v, s = nc.vector, nc.scalar
    with p.scoped() as sc:
        m2 = sc.sb(p.uid("mh"), [1, 1], F32)
        p.q(v, 1, lambda: v.tensor_mul(m2[:, :], qmax2[:, :], kmax2[:, :]))
        if take_sqrt:
            p.q(s, 1, lambda: s.activation(m2[:, :], m2[:, :], AF.Sqrt))
        if extra11 is not None:
            p.q(v, 1, lambda: v.tensor_add(m2[:, :], m2[:, :], extra11[:, :]))
        p.q(v, 1, lambda: v.tensor_scalar(m2[:, :], m2[:, :], -1.0, None, ALU.mult))
        for j in range(T // 512):
            p.q(v, 1, lambda j=j: v.tensor_scalar(
                mh_row[0:1, j * 512:(j + 1) * 512],
                c["ones_row_f"][0:1, 0:512], m2[0:1, 0:1], None, ALU.mult))


def emit_rope_build(p, pos_t, fcol64, theta, kf, ki, cosb, sinb, T):
    nc = p.nc
    v, s = nc.vector, nc.scalar
    for j in range(T // 2048):
        sl = slice(j * 2048, (j + 1) * 2048)
        p.q(v, 1, lambda sl=sl: v.tensor_scalar(
            theta[:, sl], pos_t[:, sl], fcol64[:, 0:1], None, ALU.mult))
        p.q(v, 1, lambda sl=sl: v.tensor_scalar(
            kf[:, sl], theta[:, sl], 1.0 / TWO_PI, None, ALU.mult))
        p.q(v, 1, lambda sl=sl: v.tensor_copy(ki[:, :], kf[:, sl]))
        p.q(v, 1, lambda sl=sl: v.tensor_copy(kf[:, sl], ki[:, :]))
        p.q(v, 1, lambda sl=sl: v.scalar_tensor_tensor(
            theta[:, sl], kf[:, sl], -RR_C1, theta[:, sl], ALU.mult, ALU.add))
        p.q(v, 1, lambda sl=sl: v.scalar_tensor_tensor(
            theta[:, sl], kf[:, sl], -RR_C2, theta[:, sl], ALU.mult, ALU.add))
        p.q(s, 1, lambda sl=sl: s.activation(sinb[:, sl], theta[:, sl], AF.Sin))
        p.q(v, 1, lambda sl=sl: v.tensor_scalar(
            kf[:, sl], theta[:, sl], math.pi / 2, None, ALU.is_lt))
        p.q(v, 1, lambda sl=sl: v.tensor_scalar(
            theta[:, sl], theta[:, sl], -1.5 * math.pi, None, ALU.add))
        p.q(v, 1, lambda sl=sl: v.scalar_tensor_tensor(
            theta[:, sl], kf[:, sl], TWO_PI, theta[:, sl], ALU.mult, ALU.add))
        p.q(s, 1, lambda sl=sl: s.activation(cosb[:, sl], theta[:, sl], AF.Sin))


def emit_rope_apply(p, featT, cosb, sinb, col0, T, xbuf, ybuf, swap):
    nc = p.nc
    v, sy = nc.vector, nc.sync
    cs = slice(col0, col0 + T)
    p.q(sy, 16, lambda: sy.dma_start(swap[0:32, 0:T], featT[32:64, 0:T]))
    p.q(sy, 16, lambda: sy.dma_start(swap[32:64, 0:T], featT[0:32, 0:T]))
    p.q(v, 1, lambda: v.tensor_mul(xbuf[:, 0:T], featT[0:64, 0:T], cosb[:, cs]))
    p.q(v, 1, lambda: v.tensor_mul(ybuf[:, 0:T], swap[0:64, 0:T], sinb[:, cs]))
    p.q(v, 1, lambda: v.tensor_tensor(featT[0:32, 0:T], xbuf[0:32, 0:T],
                                      ybuf[0:32, 0:T], ALU.subtract))
    p.q(v, 1, lambda: v.tensor_tensor(featT[32:64, 0:T], ybuf[32:64, 0:T],
                                      xbuf[32:64, 0:T], ALU.add))


def emit_attn_core(p, c, kT_d, qT_d, vT_d, ctxT_d, mh_row, nkt=32, rope=None,
                   maskmul=None, kv_stage2=None):
    nc = p.nc
    v, s, sy, t, g = nc.vector, nc.scalar, nc.sync, nc.tensor, nc.gpsimd
    with p.scoped() as sc:
        kq = sc.sb("at_kq", [128, SEQ], BF16)
        qq = sc.sb("at_qq", [128, TQ], BF16)
        vt_sb = sc.sb("at_vt", [64, SEQ], BF16)
        vaug = sc.sb("at_vaug", [128, nkt * 65], BF16)
        PT = sc.sb("at_PT", [128, nkt * 512], BF16)
        recip = sc.sb("at_recip", [65, 512], F32)
        ones65 = sc.sb("at_ones65", [65, 64], F32)
        ctxf = sc.sb("at_ctxf", [64, 512], F32)
        ctxo = sc.sb("at_ctxo", [64, 512], BF16)
        ps_s = [sc.ps("at_pss%d" % i, [128, 512], F32) for i in range(2)]
        ps_ctx = sc.ps("at_psctx", [128, 512], F32)
        ps_b = sc.ps("at_psb", [64, 512], F32)
        ps_vt = sc.ps("at_psvt", [128, 64], BF16)
        p.q(g, 1, lambda: g.memset(vaug[:, :], 1.0))
        p.q(g, 1, lambda: g.memset(kq[64:65, :], 1.0))
        p.q(g, 1, lambda: g.memset(ones65[:, :], 1.0))
        p.q(g, 16, lambda: g.dma_start(qq[64:65, :], mh_row[0:1, :]))

        for h in range(NH):
            pr, hh = h // 2, h % 2
            if kv_stage2 is not None:
                kv_stage2(h, kq, qq, vt_sb)
            else:
                p.q(sy, 16, lambda pr=pr, hh=hh: sy.dma_start(
                    kq[0:64, :], kT_d.ap()[hh * 64:hh * 64 + 64, pr, :]))
                p.q(sy, 16, lambda pr=pr, hh=hh: sy.dma_start(
                    qq[0:64, :], qT_d.ap()[hh * 64:hh * 64 + 64, pr, :]))
                p.q(sy, 16, lambda pr=pr, hh=hh: sy.dma_start(
                    vt_sb[:, :], vT_d.ap()[hh * 64:hh * 64 + 64, pr, :]))
            if rope is not None:
                fd = rope["freqsT_d"]
                p.q(sy, 16, lambda h=h: sy.dma_start(
                    rope["fcol"][0:32, 0:1], fd.ap()[:, h:h + 1],
                    allow_slow_non_contiguous=True))
                p.q(sy, 16, lambda h=h: sy.dma_start(
                    rope["fcol"][32:64, 0:1], fd.ap()[:, h:h + 1],
                    allow_slow_non_contiguous=True))
                emit_rope_build(p, rope["pos_t"], rope["fcol"], rope["theta"],
                                rope["kf"], rope["ki"], rope["cosb"],
                                rope["sinb"], SEQ)
                emit_rope_apply(p, kq, rope["cosb"], rope["sinb"], 0, SEQ,
                                rope["rx"], rope["ry"], rope["swap"])
                emit_rope_build(p, rope["pos_tq"], rope["fcol"], rope["theta"],
                                rope["kf"], rope["ki"], rope["cosq"],
                                rope["sinq"], TQ)
                emit_rope_apply(p, qq, rope["cosq"], rope["sinq"], 0, TQ,
                                rope["rx"], rope["ry"], rope["swap"])
            for kt in range(nkt):
                p.q(t, 1, lambda kt=kt: t.transpose(
                    ps_vt[:, :], vt_sb[:, kt * 128:(kt + 1) * 128],
                    c["ident"][0:64, 0:64]))
                p.q(v, 1, lambda kt=kt: v.tensor_copy(
                    vaug[:, kt * 65:kt * 65 + 64], ps_vt[:, :]))
            for qg in range(TQ // 512):
                qs = slice(qg * 512, (qg + 1) * 512)
                for kt in range(nkt):
                    p.q(t, 1, lambda kt=kt, qs=qs: t.matmul(
                        ps_s[kt % 2][:, :], kq[0:65, kt * 128:(kt + 1) * 128],
                        qq[0:65, qs], start=True, stop=True))
                    p.q(s, 1, lambda kt=kt: s.activation(
                        PT[:, kt * 512:(kt + 1) * 512], ps_s[kt % 2][:, :],
                        AF.Exp))
                    if maskmul is not None:
                        maskmul(PT[:, kt * 512:(kt + 1) * 512], kt, qg)
                for kt in range(nkt):
                    p.q(t, 1 if kt == nkt - 1 else 0,
                        lambda kt=kt: t.matmul(
                            ps_ctx[0:65, :], vaug[:, kt * 65:(kt + 1) * 65],
                            PT[:, kt * 512:(kt + 1) * 512],
                            start=(kt == 0), stop=(kt == nkt - 1)),
                        wait=(kt == 0))
                p.q(v, 1, lambda: v.reciprocal(recip[64:65, :],
                                               ps_ctx[64:65, :]))
                p.q(t, 1, lambda: t.matmul(
                    ps_b[:, :], ones65[64:65, 0:64], recip[64:65, :],
                    start=True, stop=True))
                p.q(v, 1, lambda: v.tensor_copy(ctxf[:, :], ps_ctx[0:64, :]))
                p.q(v, 1, lambda: v.tensor_mul(ctxo[:, :], ctxf[:, :], ps_b[:, :]))
                p.q(sy, 16, lambda qs=qs, pr=pr, hh=hh: sy.dma_start(
                    ctxT_d.ap()[hh * 64:hh * 64 + 64, pr, qs], ctxo[:, :]))



def emit_row_replicate(p, c, row_ap, out_t, T, nrows=64):
    nc = p.nc
    with p.scoped() as sc:
        pb = sc.ps(p.uid("rrep"), [nrows, 512], F32)
        for j in range(T // 512):
            p.q(nc.tensor, 1, lambda j=j: nc.tensor.matmul(
                pb[:, :], c["ones_row_f"][0:1, 0:nrows],
                row_ap[0:1, j * 512:(j + 1) * 512], start=True, stop=True))
            p.q(nc.vector, 1, lambda j=j: nc.vector.tensor_copy(
                out_t[:, j * 512:(j + 1) * 512], pb[:, :]))


def emit_final_proj_residual(p, c, w_sb, actT_d, x_own_d, y_d, bias_rep=None,
                             extra_res_d=None, din=D):
    nc = p.nc
    v, sy = nc.vector, nc.sync
    with p.scoped() as sc:
        xres = sc.sb(p.uid("fx"), [128, D], F32)
        hres = sc.sb(p.uid("fh"), [128, D], F32) if extra_res_d is not None else None
        yout = sc.sb(p.uid("fy"), [128, D], F32)

        def evac(ti, og, pp):
            if og == 0:
                p.q(sy, 16, lambda ti=ti: sy.dma_start(
                    xres[:, :], x_own_d[ti * 128:(ti + 1) * 128, :]))
                if extra_res_d is not None:
                    p.q(sy, 16, lambda ti=ti: sy.dma_start(
                        hres[:, :], extra_res_d[ti * 128:(ti + 1) * 128, :]))
            osl = slice(og * 512, (og + 1) * 512)
            if bias_rep is not None:
                p.q(v, 1, lambda osl=osl, pp=pp: v.tensor_add(
                    yout[:, osl], pp[:, :], bias_rep[:, osl]))
                p.q(v, 1, lambda osl=osl: v.tensor_add(
                    yout[:, osl], yout[:, osl], xres[:, osl]))
            else:
                p.q(v, 1, lambda osl=osl, pp=pp: v.tensor_add(
                    yout[:, osl], pp[:, :], xres[:, osl]))
            if extra_res_d is not None:
                p.q(v, 1, lambda osl=osl: v.tensor_add(
                    yout[:, osl], yout[:, osl], hres[:, osl]))
            if og == 1:
                p.q(sy, 16, lambda ti=ti: sy.dma_start(
                    y_d[ti * 128:(ti + 1) * 128, :], yout[:, :]))

        emit_proj_xs(p, w_sb, din, D, actT_d, TQ, evac)


def build_attn_std(kind):
    p = Prog()
    nc = p.nc
    v, s, sy, t, g = nc.vector, nc.scalar, nc.sync, nc.tensor, nc.gpsimd

    x_d = nc.dram_tensor("x_full", [SEQ, D], F32, kind="ExternalInput").ap()
    xo_d = nc.dram_tensor("x_own", [TQ, D], F32, kind="ExternalInput").ap()
    if kind == "L0":
        wqkv_d = nc.dram_tensor("w_qkv", [D, 3 * D], F32, kind="ExternalInput")
        bqkv_d = nc.dram_tensor("b_qkv", [128, 24], F32, kind="ExternalInput")
        wo_d = nc.dram_tensor("w_out", [D, D], F32, kind="ExternalInput")
        bo_d = nc.dram_tensor("b_out", [D], F32, kind="ExternalInput")
        freqsT_d = nc.dram_tensor("freqsT", [32, NH], F32, kind="ExternalInput")
        posk_d = nc.dram_tensor("posk_row", [1, SEQ], F32, kind="ExternalInput")
        posq_d = nc.dram_tensor("posq_row", [1, TQ], F32, kind="ExternalInput")
    else:
        wq_d = nc.dram_tensor("wq", [D, D], F32, kind="ExternalInput")
        wk_d = nc.dram_tensor("wk", [D, D], F32, kind="ExternalInput")
        wv_d = nc.dram_tensor("wv", [D, D], F32, kind="ExternalInput")
        wo_d = nc.dram_tensor("wo", [D, D], F32, kind="ExternalInput")
        bo_d = nc.dram_tensor("bo", [D], F32, kind="ExternalInput")
    y_d = nc.dram_tensor("y", [TQ, D], F32, kind="ExternalOutput").ap()

    xhatT_d = p.dram("xhatT", [128, 8, SEQ], BF16)
    xhatTq_d = p.dram("xhatTq", [128, 8, TQ], BF16)
    kT_d = p.dram("kT", [128, 8, SEQ], BF16)
    vT_d = p.dram("vT", [128, 8, SEQ], BF16)
    qT_d = p.dram("qT", [128, 8, TQ], BF16)
    ctxT_d = p.dram("ctxT", [128, 8, TQ], BF16)

    c = setup_consts(p)
    emit_ln_transpose(p, c, x_d, 0, SEQ, xhatT_d)
    emit_ln_transpose(p, c, xo_d, 0, TQ, xhatTq_d)

    qscale = 1.0 / math.sqrt(DH)
    if kind == "L0":
        with p.scoped() as sc:
            bias_cols = load_bias_cols(p, sc, bqkv_d.ap(), 3 * D)
            w_all = load_weight_ws(p, sc, wqkv_d.ap(), D, 3 * D)
            emit_proj_ws(p, w_all, D, D, xhatT_d, SEQ, kT_d, w_dout=3 * D,
                         w_col0=D, bias_cols=bias_cols, bias_off=8)
            emit_proj_ws(p, w_all, D, D, xhatT_d, SEQ, vT_d, w_dout=3 * D,
                         w_col0=2 * D, bias_cols=bias_cols, bias_off=16)
            emit_proj_ws(p, w_all, D, D, xhatTq_d, TQ, qT_d, w_dout=3 * D,
                         w_col0=0, bias_cols=bias_cols, bias_off=0, scale=qscale)
    else:
        with p.scoped() as sc:
            wk_sb = load_weight_ws(p, sc, wk_d.ap(), D, D)
            emit_proj_ws(p, wk_sb, D, D, xhatT_d, SEQ, kT_d)
        with p.scoped() as sc:
            wv_sb = load_weight_ws(p, sc, wv_d.ap(), D, D)
            emit_proj_ws(p, wv_sb, D, D, xhatT_d, SEQ, vT_d)
        with p.scoped() as sc:
            wq_sb = load_weight_ws(p, sc, wq_d.ap(), D, D)
            emit_proj_ws(p, wq_sb, D, D, xhatTq_d, TQ, qT_d, scale=qscale)

    qmax2 = p.sb("qmax2", [1, 1], F32)
    kmax2 = p.sb("kmax2", [1, 1], F32)
    mh_row = p.sb("mh_row", [1, TQ], F32)
    emit_norm2_max(p, c, kT_d, range(8), SEQ, kmax2)
    emit_norm2_max(p, c, qT_d, range(8), TQ, qmax2)
    emit_mhat_row(p, c, qmax2, kmax2, mh_row, TQ)

    rope = None
    rope_scope = None
    if kind == "L0":
        rope_scope = p.scoped()
        sc = rope_scope.__enter__()
        posk_row = sc.sb("rp_poskr", [1, SEQ], F32)
        posq_row = sc.sb("rp_posqr", [1, TQ], F32)
        rope = {
            "pos_t": sc.sb("rp_pos", [64, SEQ], F32),
            "pos_tq": sc.sb("rp_posq", [64, TQ], F32),
            "freqsT_d": freqsT_d,
            "fcol": sc.sb("rp_fcol", [64, 1], F32),
            "theta": sc.sb("rp_theta", [64, SEQ], F32),
            "kf": sc.sb("rp_kf", [64, SEQ], F32),
            "ki": sc.sb("rp_ki", [64, 2048], I32),
            "cosb": sc.sb("rp_cos", [64, SEQ], BF16),
            "sinb": sc.sb("rp_sin", [64, SEQ], BF16),
            "cosq": sc.sb("rp_cosq", [64, TQ], BF16),
            "sinq": sc.sb("rp_sinq", [64, TQ], BF16),
            "rx": sc.sb("rp_rx", [64, SEQ], BF16),
            "ry": sc.sb("rp_ry", [64, SEQ], BF16),
            "swap": sc.sb("rp_swap", [64, SEQ], BF16),
        }
        p.q(sy, 16, lambda: sy.dma_start(posk_row[:, :], posk_d.ap()))
        p.q(sy, 16, lambda: sy.dma_start(posq_row[:, :], posq_d.ap()))
        emit_row_replicate(p, c, posk_row, rope["pos_t"], SEQ)
        emit_row_replicate(p, c, posq_row, rope["pos_tq"], TQ)

    emit_attn_core(p, c, kT_d, qT_d, vT_d, ctxT_d, mh_row, rope=rope)
    if rope_scope is not None:
        rope_scope.__exit__(None, None, None)

    with p.scoped() as sc:
        wo_sb = load_weight_ws(p, sc, wo_d.ap(), D, D)
        bo_rep = emit_bias_replicate(p, c, sc, bo_d.ap())
        emit_final_proj_residual(p, c, wo_sb, ctxT_d, xo_d, y_d, bias_rep=bo_rep)

    p.finish()
    return p



def emit_val_replicate(p, c, sc, src_ap, n, name=None):
    nc = p.nc
    name = name or p.uid("valr")
    row = sc.sb(name + "_row", [1, n], F32)
    cols = sc.sb(name + "_cols", [128, n], F32)
    p.q(nc.sync, 16, lambda: nc.sync.dma_start(row[:, :], src_ap))
    with p.scoped() as sc2:
        pb = sc2.ps(name + "_ps", [128, n], F32)
        p.q(nc.tensor, 1, lambda: nc.tensor.matmul(
            pb[:, :], c["ones_row_f"][0:1, 0:128], row[0:1, :],
            start=True, stop=True))
        p.q(nc.vector, 1, lambda: nc.vector.tensor_copy(cols[:, :], pb[:, :]))
    return cols


def emit_hidden_ws(p, w_sb, din, dhid, xin, ncols, hc_out, hc, act=None,
                   bias_cols=None, bias_off=0, extra_mm=None, post=None):
    nc = p.nc
    v, s, t = nc.vector, nc.scalar, nc.tensor
    ndc = din // 128
    pp = hc_out["pp"]
    for dc in range(ndc):
        last = (dc == ndc - 1) and extra_mm is None
        p.q(t, 1 if last else 0, lambda dc=dc, last=last: t.matmul(
            pp[:, :], w_sb[:, dc * dhid + hc * 128:dc * dhid + hc * 128 + 128],
            xin[:, dc * ncols:(dc + 1) * ncols],
            start=(dc == 0), stop=last), wait=(dc == 0))
    if extra_mm is not None:
        extra_mm(pp)
    if post is not None:
        post(pp)
        return
    osl = slice(hc * ncols, (hc + 1) * ncols)
    bias_ap = (bias_cols[:, bias_off + hc:bias_off + hc + 1]
               if bias_cols is not None else None)
    if act is not None:
        p.q(s, 1, lambda bias_ap=bias_ap: s.activation(
            hc_out["buf"][:, osl], pp[:, :], act,
            bias=bias_ap if bias_ap is not None else 0.0))
    elif bias_ap is not None:
        p.q(v, 1, lambda bias_ap=bias_ap: v.tensor_scalar(
            hc_out["buf"][:, osl], pp[:, :], bias_ap, None, ALU.add))
    else:
        p.q(v, 1, lambda: v.tensor_copy(hc_out["buf"][:, osl], pp[:, :]))


def emit_down_xs_acc(p, wdown_sb, dhid, h_sb, nb, evac):
    nc = p.nc
    t = nc.tensor
    nhc = dhid // 128
    with p.scoped() as sc:
        pp = sc.ps(p.uid("dx"), [128, 512], F32)
        for ti in range(4):
            for og in range(2):
                for hc in range(nhc):
                    p.q(t, 1 if hc == nhc - 1 else 0,
                        lambda hc=hc, ti=ti, og=og: t.matmul(
                            pp[:, :],
                            h_sb[:, hc * 512 + ti * 128:hc * 512 + ti * 128 + 128],
                            wdown_sb[:, hc * D + og * 512:hc * D + og * 512 + 512],
                            start=(hc == 0), stop=(hc == nhc - 1)),
                        wait=(hc == 0))
                evac(ti, og, pp)


def build_l0_ffn():
    p = Prog()
    nc = p.nc
    v, s, sy, t, g = nc.vector, nc.scalar, nc.sync, nc.tensor, nc.gpsimd
    xo_d = nc.dram_tensor("x_own", [TQ, D], F32, kind="ExternalInput").ap()
    wiT_d = nc.dram_tensor("wiT", [D, 4096], F32, kind="ExternalInput")
    woT_d = nc.dram_tensor("woT", [4096, D], F32, kind="ExternalInput")
    AsT_d = nc.dram_tensor("AsT", [8, D, 16], F32, kind="ExternalInput")
    BsT_d = nc.dram_tensor("BsT", [8, 16, 4096], F32, kind="ExternalInput")
    idx_d = nc.dram_tensor("topk_idx", [1, 2], I32, kind="ExternalInput")
    val_d = nc.dram_tensor("topk_val", [1, 2], F32, kind="ExternalInput")
    y_d = nc.dram_tensor("y", [TQ, D], F32, kind="ExternalOutput").ap()

    xhatT_d = p.dram("xhatT", [128, 8, TQ], BF16)
    hT_dram = p.dram("hT", [128, 32, TQ], BF16)
    c = setup_consts(p)
    emit_ln_transpose(p, c, xo_d, 0, TQ, xhatT_d)

    with p.scoped() as sc:
        val_cols = emit_val_replicate(p, c, sc, val_d.ap(), 2)
        idx_sb = sc.sb("idx", [1, 2], I32)
        p.q(sy, 16, lambda: sy.dma_start(idx_sb[:, :], idx_d.ap()))
        A_sb = [sc.sb("A%d" % k, [128, 8 * 16], BF16) for k in range(2)]
        B_sb = [sc.sb("B%d" % k, [16, 4096], BF16) for k in range(2)]
        for k in range(2):
            with g.register("ri%d" % k) as ri:
                p.q(g, 0, lambda k=k, ri=ri: g.reg_load(ri, idx_sb[0:1, k:k + 1]))
                g.reg_mul(ri, ri, D * 16)
                p.q(g, 16, lambda k=k, ri=ri: g.dma_start(
                    A_sb[k].rearrange("p (dc r) -> p dc r", dc=8),
                    bass.AP(AsT_d, ri, [[16, 128], [16 * 128, 8], [1, 16]])),
                    wait=False)
            with g.register("rj%d" % k) as rj:
                p.q(g, 0, lambda k=k, rj=rj: g.reg_load(rj, idx_sb[0:1, k:k + 1]))
                g.reg_mul(rj, rj, 16 * 4096)
                p.q(g, 16, lambda k=k, rj=rj: g.dma_start(
                    B_sb[k][:, :],
                    bass.AP(BsT_d, rj, [[4096, 16], [1, 4096]])), wait=False)
        pk_sb = [sc.sb("pk%d" % k, [16, TQ], BF16) for k in range(2)]
        p0n_sb = sc.sb("p0n", [16, TQ], BF16)
        xin = sc.sb("l0f_xin", [128, 8 * 512], BF16)
        with p.scoped() as sc2:
            ppk = sc2.ps("ppk", [16, 512], F32)
            for n in range(TQ // 512):
                p.q(sy, 16, lambda n=n: sy.dma_start(
                    xin.rearrange("p (dc n) -> p dc n", dc=8),
                    xhatT_d.ap()[:, :, n * 512:(n + 1) * 512]))
                for k in range(2):
                    for dc in range(8):
                        p.q(t, 1 if dc == 7 else 0, lambda k=k, dc=dc: t.matmul(
                            ppk[:, :], A_sb[k][:, dc * 16:(dc + 1) * 16],
                            xin[:, dc * 512:(dc + 1) * 512],
                            start=(dc == 0), stop=(dc == 7)), wait=(dc == 0))
                    p.q(v, 1, lambda k=k, n=n: v.tensor_copy(
                        pk_sb[k][:, n * 512:(n + 1) * 512], ppk[0:16, :]))
            p.q(v, 1, lambda: v.tensor_scalar(p0n_sb[:, :], pk_sb[0][:, :],
                                              -1.0, None, ALU.mult))
        wi_sb = load_weight_ws(p, sc, wiT_d.ap(), D, 4096)
        h_sb = sc.sb("l0f_h", [128, 32 * 512], BF16)
        h0 = sc.sb("l0f_h0", [128, 512], BF16)
        h1 = sc.sb("l0f_h1", [128, 512], BF16)
        with p.scoped() as sc2:
            pp = sc2.ps("l0f_pp", [128, 512], F32)
            hco = {"pp": pp, "buf": h_sb}
            for n in range(TQ // 512):
                ns = slice(n * 512, (n + 1) * 512)
                p.q(sy, 16, lambda n=n: sy.dma_start(
                    xin.rearrange("p (dc n) -> p dc n", dc=8),
                    xhatT_d.ap()[:, :, n * 512:(n + 1) * 512]))
                for hc in range(32):
                    def lora0(pp, hc=hc, ns=ns):
                        p.q(t, 1, lambda: t.matmul(
                            pp[:, :], B_sb[0][:, hc * 128:(hc + 1) * 128],
                            pk_sb[0][0:16, ns], start=False, stop=True,
                            skip_group_check=True), wait=False)
                    emit_hidden_ws(p, wi_sb, D, 4096, xin, 512, hco, hc,
                                   extra_mm=lora0,
                                   post=lambda pp2: p.q(s, 1, lambda: s.activation(
                                       h0[:, :], pp2[:, :], AF.Relu)))
                    p.q(t, 0, lambda hc=hc, ns=ns: t.matmul(
                        pp[:, :], B_sb[1][:, hc * 128:(hc + 1) * 128],
                        pk_sb[1][0:16, ns], start=False, stop=False,
                        skip_group_check=True))
                    p.q(t, 1, lambda hc=hc, ns=ns: t.matmul(
                        pp[:, :], B_sb[0][:, hc * 128:(hc + 1) * 128],
                        p0n_sb[0:16, ns], start=False, stop=True,
                        skip_group_check=True), wait=False)
                    p.q(s, 1, lambda: s.activation(h1[:, :], pp[:, :], AF.Relu))
                    p.q(v, 1, lambda: v.tensor_scalar(
                        h1[:, :], h1[:, :], val_cols[:, 1:2], None, ALU.mult))
                    p.q(v, 1, lambda hc=hc: v.scalar_tensor_tensor(
                        h_sb[:, hc * 512:(hc + 1) * 512], h0[:, :],
                        val_cols[:, 0:1], h1[:, :], ALU.mult, ALU.add))
                p.q(sy, 16, lambda n=n: sy.dma_start(
                    hT_dram.ap()[:, :, n * 512:(n + 1) * 512],
                    h_sb.rearrange("p (hc n) -> p hc n", hc=32)))

    with p.scoped() as sc:
        wo_sb = load_weight_ws(p, sc, woT_d.ap(), 4096, D)
        h_sb = sc.sb("l0f_hb", [128, 32 * 512], BF16)
        xres = sc.sb("l0f_xres", [128, D], F32)
        yout = sc.sb("l0f_y", [128, D], F32)
        for n in range(TQ // 512):
            p.q(sy, 16, lambda n=n: sy.dma_start(
                h_sb.rearrange("p (hc n) -> p hc n", hc=32),
                hT_dram.ap()[:, :, n * 512:(n + 1) * 512]))

            def evac(ti, og, ppd, n=n):
                gt = n * 4 + ti
                if og == 0:
                    p.q(sy, 16, lambda gt=gt: sy.dma_start(
                        xres[:, :], xo_d[gt * 128:(gt + 1) * 128, :]))
                osl = slice(og * 512, (og + 1) * 512)
                p.q(v, 1, lambda osl=osl, ppd=ppd: v.tensor_add(
                    yout[:, osl], ppd[:, :], xres[:, osl]))
                if og == 1:
                    p.q(sy, 16, lambda gt=gt: sy.dma_start(
                        y_d[gt * 128:(gt + 1) * 128, :], yout[:, :]))

            emit_down_xs_acc(p, wo_sb, 4096, h_sb, n, evac)
    p.finish()
    return p


def emit_router_gates(p, c, sc, xhatT_d, rT_sb, gate_kind, w_tok_d,
                      rbias_col=None):
    nc = p.nc
    v, s, sy, t = nc.vector, nc.scalar, nc.sync, nc.tensor
    lgT = sc.sb(p.uid("lgT"), [8, TQ], F32)
    xin = sc.sb(p.uid("rgx"), [128, 8 * 512], BF16)
    with p.scoped() as sc2:
        pr = sc2.ps(p.uid("pr"), [8, 512], F32)
        for n in range(TQ // 512):
            p.q(sy, 16, lambda n=n: sy.dma_start(
                xin.rearrange("p (dc n) -> p dc n", dc=8),
                xhatT_d.ap()[:, :, n * 512:(n + 1) * 512]))
            for dc in range(8):
                p.q(t, 1 if dc == 7 else 0, lambda dc=dc: t.matmul(
                    pr[:, :], rT_sb[:, dc * 8:(dc + 1) * 8],
                    xin[:, dc * 512:(dc + 1) * 512],
                    start=(dc == 0), stop=(dc == 7)), wait=(dc == 0))
            if rbias_col is not None:
                p.q(v, 1, lambda n=n: v.tensor_scalar(
                    lgT[:, n * 512:(n + 1) * 512], pr[0:8, :],
                    rbias_col[0:8, 0:1], None, ALU.add))
            else:
                p.q(v, 1, lambda n=n: v.tensor_copy(
                    lgT[:, n * 512:(n + 1) * 512], pr[0:8, :]))
    lg = sc.sb(p.uid("lg"), [128, 8], F32)
    m1 = sc.sb(p.uid("m1"), [128, 1], F32)
    m2 = sc.sb(p.uid("m2"), [128, 1], F32)
    nm1 = sc.sb(p.uid("nm1"), [128, 1], F32)
    tmp8 = sc.sb(p.uid("t8"), [128, 8], F32)
    sel = sc.sb(p.uid("sel"), [128, 8], F32)
    wv8 = sc.sb(p.uid("wv8"), [128, 8], F32)
    den = sc.sb(p.uid("den"), [128, 1], F32)
    with p.scoped() as sc2:
        ptr = sc2.ps(p.uid("ptr"), [128, 8], F32)
        for ti in range(TQ // 128):
            p.q(t, 1, lambda ti=ti: t.transpose(
                ptr[:, :], lgT[0:8, ti * 128:(ti + 1) * 128],
                c["ident_f"][0:8, 0:8]))
            p.q(v, 1, lambda: v.tensor_copy(lg[:, :], ptr[:, :]))
            p.q(v, 1, lambda: v.reduce_max(m1[:, :], lg[:, :], axis=AX))
            p.q(v, 1, lambda: v.tensor_scalar(tmp8[:, :], lg[:, :],
                                              m1[:, 0:1], None, ALU.is_ge))
            p.q(v, 1, lambda: v.scalar_tensor_tensor(
                tmp8[:, :], tmp8[:, :], -1e30, lg[:, :], ALU.mult, ALU.add))
            p.q(v, 1, lambda: v.reduce_max(m2[:, :], tmp8[:, :], axis=AX))
            p.q(v, 1, lambda: v.tensor_scalar(sel[:, :], lg[:, :],
                                              m2[:, 0:1], None, ALU.is_ge))
            if gate_kind == "sigmoid":
                p.q(s, 1, lambda: s.activation(wv8[:, :], lg[:, :], AF.Sigmoid))
                p.q(s, 1, lambda: s.activation(den[:, :], m1[:, :], AF.Sigmoid))
                p.q(s, 1, lambda: s.activation(m2[:, :], m2[:, :], AF.Sigmoid))
                p.q(v, 1, lambda: v.tensor_add(den[:, :], den[:, :], m2[:, :]))
            else:
                p.q(v, 1, lambda: v.tensor_scalar(nm1[:, :], m1[:, :], -1.0,
                                                  None, ALU.mult))
                p.q(s, 1, lambda: s.activation(wv8[:, :], lg[:, :], AF.Exp,
                                               bias=nm1[:, 0:1]))
                if gate_kind == "softmax_renorm":
                    p.q(v, 1, lambda: v.tensor_sub(den[:, :], m2[:, :], m1[:, :]))
                    p.q(s, 1, lambda: s.activation(den[:, :], den[:, :], AF.Exp))
                    p.q(v, 1, lambda: v.tensor_scalar(den[:, :], den[:, :], 1.0,
                                                      None, ALU.add))
                else:
                    p.q(v, 1, lambda: v.reduce_sum(den[:, :], wv8[:, :], axis=AX))
            p.q(v, 1, lambda: v.reciprocal(den[:, :], den[:, :]))
            p.q(v, 1, lambda: v.tensor_mul(wv8[:, :], wv8[:, :], sel[:, :]))
            p.q(v, 1, lambda: v.tensor_scalar(wv8[:, :], wv8[:, :],
                                              den[:, 0:1], None, ALU.mult))
            p.q(sy, 16, lambda ti=ti: sy.dma_start(
                w_tok_d[ti * 128:(ti + 1) * 128, :], wv8[:, :]))


def emit_tile_transpose_pass(p, c, src_tok_d, T, dstT_d):
    nc = p.nc
    v, sy, t = nc.vector, nc.sync, nc.tensor
    with p.scoped() as sc:
        xt = sc.sb(p.uid("ttp_x"), [128, D], BF16)
        xT = sc.sb(p.uid("ttp_xT"), [128, D], BF16)
        ptr = sc.ps(p.uid("ttp_ps"), [128, 128], BF16)
        for i in range(T // 128):
            p.q(sy, 16, lambda i=i: sy.dma_start(
                xt[:, :], src_tok_d[i * 128:(i + 1) * 128, :]))
            for dc in range(8):
                p.q(t, 1, lambda dc=dc: t.transpose(
                    ptr[:, :], xt[:, dc * 128:(dc + 1) * 128], c["ident"][:, :]))
                p.q(v, 1, lambda dc=dc: v.tensor_copy(
                    xT[:, dc * 128:(dc + 1) * 128], ptr[:, :]))
            p.q(sy, 16, lambda i=i: sy.dma_start(
                dstT_d.ap()[:, :, i * 128:(i + 1) * 128],
                xT.rearrange("p (dc n) -> p dc n", dc=8)))


def build_moe_ffn(kind):
    p = Prog()
    nc = p.nc
    v, s, sy, t, g = nc.vector, nc.scalar, nc.sync, nc.tensor, nc.gpsimd
    xo_d = nc.dram_tensor("x_own", [TQ, D], F32, kind="ExternalInput").ap()
    y_d = nc.dram_tensor("y", [TQ, D], F32, kind="ExternalOutput").ap()
    if kind == "L1":
        HID, NE = 2048, 8
        sgT_d = nc.dram_tensor("sgT", [D, HID], F32, kind="ExternalInput")
        suT_d = nc.dram_tensor("suT", [D, HID], F32, kind="ExternalInput")
        sdT_d = nc.dram_tensor("sdT", [HID, D], F32, kind="ExternalInput")
        rT_d = nc.dram_tensor("rT", [D, 8], F32, kind="ExternalInput")
        rb_d = nc.dram_tensor("rb", [8, 1], F32, kind="ExternalInput")
        wgT_d = nc.dram_tensor("wgT", [NE, D, HID], F32, kind="ExternalInput")
        wuT_d = nc.dram_tensor("wuT", [NE, D, HID], F32, kind="ExternalInput")
        wdT_d = nc.dram_tensor("wdT", [NE, HID, D], F32, kind="ExternalInput")
    elif kind == "L2":
        HID, NE = 4096, 8
        sgT_d = nc.dram_tensor("WfsT", [D, HID], F32, kind="ExternalInput")
        bfs_d = nc.dram_tensor("bfs_cols", [128, HID // 128], F32,
                               kind="ExternalInput")
        sdT_d = nc.dram_tensor("WpsT", [HID, D], F32, kind="ExternalInput")
        bps_d = nc.dram_tensor("bps", [D], F32, kind="ExternalInput")
        rT_d = nc.dram_tensor("WrT", [D, 8], F32, kind="ExternalInput")
        weT_d = nc.dram_tensor("WfcT", [NE, D, HID], F32, kind="ExternalInput")
        bfc_d = nc.dram_tensor("bfc_cols", [NE, 128, HID // 128], F32,
                               kind="ExternalInput")
        wpT_d = nc.dram_tensor("WprT", [NE, HID, D], F32, kind="ExternalInput")
        bpr_d = nc.dram_tensor("bpr", [NE, D], F32, kind="ExternalInput")
    else:
        HID, NE = 2816, 8
        rT_d = nc.dram_tensor("gateT", [D, 8], F32, kind="ExternalInput")
        w1T_d = nc.dram_tensor("w1T", [NE, D, HID], F32, kind="ExternalInput")
        w3T_d = nc.dram_tensor("w3T", [NE, D, HID], F32, kind="ExternalInput")
        w2T_d = nc.dram_tensor("w2T", [NE, HID, D], F32, kind="ExternalInput")

    xhatT_d = p.dram("xhatT", [128, 8, TQ], BF16)
    xtok_d = p.dram("xtok", [TQ, D], BF16) if kind == "L3" else None
    w_tok_d = p.dram("w_tok", [TQ, 8], F32)
    c = setup_consts(p)
    emit_ln_transpose(p, c, xo_d, 0, TQ, xhatT_d,
                      tokmajor_d=xtok_d.ap() if xtok_d is not None else None)

    gate_kind = {"L1": "sigmoid", "L2": "softmax_renorm", "L3": "softmax"}[kind]
    with p.scoped() as sc:
        rT_sb = load_weight_ws(p, sc, rT_d.ap(), D, 8)
        rb_col = None
        if kind == "L1":
            rb_col = sc.sb("rbcol", [8, 1], F32)
            p.q(sy, 16, lambda: sy.dma_start(rb_col[:, :], rb_d.ap()))
        emit_router_gates(p, c, sc, xhatT_d, rT_sb, gate_kind, w_tok_d.ap(),
                          rbias_col=rb_col)

    nhc = HID // 128
    xin = p.sb("mf_xin", [128, 8 * 512], BF16)
    h_sb = p.sb("mf_h", [128, nhc * 512], BF16)
    gs = p.sb("mf_gs", [128, 512], BF16)
    xres = p.sb("mf_xres", [128, D], F32)
    yout = p.sb("mf_y", [128, D], F32)
    wcol8 = p.sb("mf_w8", [128, 8], F32)

    def hidden_swiglu(wg_sb, wu_sb, n, xin_sb, bias_cols=None):
        with p.scoped() as sc2:
            pp = sc2.ps(p.uid("hsw"), [128, 512], F32)
            hco = {"pp": pp, "buf": h_sb}
            for hc in range(nhc):
                emit_hidden_ws(p, wg_sb, D, HID, xin_sb, 512, hco, hc,
                               post=lambda pp2: p.q(s, 1, lambda: s.activation(
                                   gs[:, :], pp2[:, :], AF.Silu)))
                for dc in range(8):
                    p.q(t, 1 if dc == 7 else 0, lambda hc=hc, dc=dc: t.matmul(
                        pp[:, :],
                        wu_sb[:, dc * HID + hc * 128:dc * HID + hc * 128 + 128],
                        xin_sb[:, dc * 512:(dc + 1) * 512],
                        start=(dc == 0), stop=(dc == 7)), wait=(dc == 0))
                p.q(v, 1, lambda hc=hc: v.tensor_tensor(
                    h_sb[:, hc * 512:(hc + 1) * 512], gs[:, :], pp[:, :],
                    ALU.mult))

    def hidden_gelu(we_sb, n, xin_sb, bias_cols):
        with p.scoped() as sc2:
            pp = sc2.ps(p.uid("hge"), [128, 512], F32)
            hco = {"pp": pp, "buf": h_sb}
            for hc in range(nhc):
                emit_hidden_ws(p, we_sb, D, HID, xin_sb, 512, hco, hc,
                               act=AF.Gelu_apprx_tanh, bias_cols=bias_cols)

    def down_acc(wd_sb, n, e=None, init=False, brep=None):
        def evac(ti, og, pp):
            gt = n * 4 + ti
            osl = slice(og * 512, (og + 1) * 512)
            if og == 0:
                if init:
                    p.q(sy, 16, lambda gt=gt: sy.dma_start(
                        xres[:, :], xo_d[gt * 128:(gt + 1) * 128, :]))
                else:
                    p.q(sy, 16, lambda gt=gt: sy.dma_start(
                        xres[:, :], y_d[gt * 128:(gt + 1) * 128, :]))
                if e is not None:
                    p.q(sy, 16, lambda gt=gt: sy.dma_start(
                        wcol8[:, :], w_tok_d.ap()[gt * 128:(gt + 1) * 128, :]))
            if e is None:
                if brep is not None:
                    p.q(v, 1, lambda osl=osl, pp=pp: v.tensor_add(
                        yout[:, osl], pp[:, :], brep[:, osl]))
                    p.q(v, 1, lambda osl=osl: v.tensor_add(
                        yout[:, osl], yout[:, osl], xres[:, osl]))
                else:
                    p.q(v, 1, lambda osl=osl, pp=pp: v.tensor_add(
                        yout[:, osl], pp[:, :], xres[:, osl]))
            else:
                if brep is not None:
                    p.q(v, 1, lambda osl=osl, pp=pp: v.tensor_add(
                        yout[:, osl], pp[:, :], brep[:, osl]))
                    p.q(v, 1, lambda osl=osl, e=e: v.scalar_tensor_tensor(
                        yout[:, osl], yout[:, osl], wcol8[:, e:e + 1],
                        xres[:, osl], ALU.mult, ALU.add))
                elif kind == "L3":
                    p.q(v, 1, lambda osl=osl, pp=pp: v.tensor_add(
                        yout[:, osl], pp[:, :], xres[:, osl]))
                else:
                    p.q(v, 1, lambda osl=osl, pp=pp, e=e: v.scalar_tensor_tensor(
                        yout[:, osl], pp[:, :], wcol8[:, e:e + 1],
                        xres[:, osl], ALU.mult, ALU.add))
            if og == 1:
                p.q(sy, 16, lambda gt=gt: sy.dma_start(
                    y_d[gt * 128:(gt + 1) * 128, :], yout[:, :]))
        emit_down_xs_acc(p, wd_sb, HID, h_sb, n, evac)

    if kind in ("L1", "L2"):
        with p.scoped() as sc:
            if kind == "L1":
                wg_sb = load_weight_ws(p, sc, sgT_d.ap(), D, HID)
                wu_sb = load_weight_ws(p, sc, suT_d.ap(), D, HID)
            else:
                wg_sb = load_weight_ws(p, sc, sgT_d.ap(), D, HID)
                bfs_sb = load_bias_cols(p, sc, bfs_d.ap(), HID)
            wd_sb = load_weight_ws(p, sc, sdT_d.ap(), HID, D)
            brep = None
            if kind == "L2":
                brep = emit_bias_replicate(p, c, sc, bps_d.ap())
            for n in range(TQ // 512):
                p.q(sy, 16, lambda n=n: sy.dma_start(
                    xin.rearrange("p (dc n) -> p dc n", dc=8),
                    xhatT_d.ap()[:, :, n * 512:(n + 1) * 512]))
                if kind == "L1":
                    hidden_swiglu(wg_sb, wu_sb, n, xin)
                else:
                    hidden_gelu(wg_sb, n, xin, bfs_sb)
                down_acc(wd_sb, n, e=None, init=True, brep=brep)

    if kind == "L3":
        p.q(sy, 16, lambda: sy.dma_start(y_d[:, :], xo_d[:, :]))

    xe_tok = p.sb("mf_xe", [128, D], BF16) if kind == "L3" else None
    xeT = p.sb("mf_xeT", [128, 8 * 512], BF16) if kind == "L3" else None
    for e in range(NE):
        with p.scoped() as sc:
            if kind == "L1":
                wg_sb = load_weight_ws(p, sc, wgT_d.ap()[e], D, HID)
                wu_sb = load_weight_ws(p, sc, wuT_d.ap()[e], D, HID)
                wd_sb = load_weight_ws(p, sc, wdT_d.ap()[e], HID, D)
                brep = None
                bcols = None
            elif kind == "L2":
                wg_sb = load_weight_ws(p, sc, weT_d.ap()[e], D, HID)
                bcols = sc.sb(p.uid("bfce"), [128, HID // 128], F32)
                p.q(sy, 16, lambda e=e: sy.dma_start(bcols[:, :],
                                                     bfc_d.ap()[e]))
                wd_sb = load_weight_ws(p, sc, wpT_d.ap()[e], HID, D)
                brep = emit_bias_replicate(p, c, sc, bpr_d.ap()[e])
            else:
                wg_sb = load_weight_ws(p, sc, w1T_d.ap()[e], D, HID)
                wu_sb = load_weight_ws(p, sc, w3T_d.ap()[e], D, HID)
                wd_sb = load_weight_ws(p, sc, w2T_d.ap()[e], HID, D)
                brep = None
            with p.scoped() as sc2:
                ptr = sc2.ps(p.uid("xe_ps"), [128, 128], BF16)
                for n in range(TQ // 512):
                    if kind == "L3":
                        for ti in range(4):
                            gt = n * 4 + ti
                            p.q(sy, 16, lambda gt=gt: sy.dma_start(
                                xe_tok[:, :], xtok_d.ap()[gt * 128:(gt + 1) * 128, :]))
                            p.q(sy, 16, lambda gt=gt: sy.dma_start(
                                wcol8[:, :], w_tok_d.ap()[gt * 128:(gt + 1) * 128, :]))
                            p.q(v, 1, lambda e=e: v.tensor_scalar(
                                xe_tok[:, :], xe_tok[:, :], wcol8[:, e:e + 1],
                                None, ALU.mult))
                            for dc in range(8):
                                p.q(t, 1, lambda dc=dc: t.transpose(
                                    ptr[:, :], xe_tok[:, dc * 128:(dc + 1) * 128],
                                    c["ident"][:, :]))
                                p.q(v, 1, lambda dc=dc, ti=ti: v.tensor_copy(
                                    xeT[:, dc * 512 + ti * 128:
                                        dc * 512 + ti * 128 + 128], ptr[:, :]))
                        xin_use = xeT
                    else:
                        p.q(sy, 16, lambda n=n: sy.dma_start(
                            xin.rearrange("p (dc n) -> p dc n", dc=8),
                            xhatT_d.ap()[:, :, n * 512:(n + 1) * 512]))
                        xin_use = xin
                    if kind == "L2":
                        hidden_gelu(wg_sb, n, xin_use, bcols)
                    else:
                        hidden_swiglu(wg_sb, wu_sb, n, xin_use)
                    down_acc(wd_sb, n, e=e, brep=brep)
    p.finish()
    return p



def build_l1_attn():
    p = Prog()
    nc = p.nc
    v, s, sy, t, g = nc.vector, nc.scalar, nc.sync, nc.tensor, nc.gpsimd
    x_d = nc.dram_tensor("x_full", [SEQ, D], F32, kind="ExternalInput").ap()
    xo_d = nc.dram_tensor("x_own", [TQ, D], F32, kind="ExternalInput").ap()
    wqT_d = nc.dram_tensor("wqT_own", [D, 512], F32, kind="ExternalInput")
    wkT_d = nc.dram_tensor("wkT", [D, D], F32, kind="ExternalInput")
    wvT_d = nc.dram_tensor("wvT", [D, D], F32, kind="ExternalInput")
    woT_d = nc.dram_tensor("woT", [D, D], F32, kind="ExternalInput")
    bo_d = nc.dram_tensor("bo", [D], F32, kind="ExternalInput")
    y_d = nc.dram_tensor("y", [TQ, D], F32, kind="ExternalOutput").ap()

    xhatT_d = p.dram("xhatT", [128, 8, SEQ], BF16)
    q_tok_d = p.dram("q_tok", [SEQ, 512], BF16)
    k_tok_d = p.dram("k_tok", [SEQ, D], BF16)
    v_tok_d = p.dram("v_tok", [SEQ, D], BF16)
    outh_d = p.dram("outh", [8, SEQ, DH], BF16)
    orT_d = p.dram("orT", [128, 8, TQ], BF16)

    c = setup_consts(p)
    emit_ln_transpose(p, c, x_d, 0, SEQ, xhatT_d)

    ob = p.sb("l1_ob", [128, 512], BF16)

    def tokproj(w_d, dout, out_d, scale=None):
        with p.scoped() as sc:
            w_sb = load_weight_ws(p, sc, w_d.ap(), D, dout)

            def evac(ti, og, pp):
                ow = min(512, dout)
                if scale is not None:
                    p.q(v, 1, lambda pp=pp, ow=ow: v.tensor_scalar(
                        ob[:, 0:ow], pp[:, :], scale, None, ALU.mult))
                else:
                    p.q(v, 1, lambda pp=pp, ow=ow: v.tensor_copy(
                        ob[:, 0:ow], pp[:, :]))
                p.q(sy, 16, lambda ti=ti, og=og, ow=ow: sy.dma_start(
                    out_d.ap()[ti * 128:(ti + 1) * 128,
                               og * 512:og * 512 + ow], ob[:, 0:ow]))

            emit_proj_xs(p, w_sb, D, dout, xhatT_d, SEQ, evac)

    tokproj(wqT_d, 512, q_tok_d, scale=1.0 / math.sqrt(DH))
    tokproj(wkT_d, D, k_tok_d)
    tokproj(wvT_d, D, v_tok_d)

    with p.scoped() as sc:
        q_t = sc.sb("l1_q", [128, 512], BF16)
        k_t = sc.sb("l1_k", [128, D], BF16)
        v_t = sc.sb("l1_v", [128, D], BF16)
        att = sc.sb("l1_att", [128, 128], F32)
        scr = sc.sb("l1_scr", [128, 64], F32)
        pa = sc.sb("l1_pa", [128, 16], F32)
        m1 = sc.sb("l1_m1", [128, 1], F32)
        nm1 = sc.sb("l1_nm1", [128, 1], F32)
        ssum = sc.sb("l1_ss", [128, 1], F32)
        acc64 = sc.sb("l1_acc", [128, 64], F32)
        o64 = sc.sb("l1_o64", [128, 64], BF16)
        for ti in range(SEQ // 128):
            p.q(sy, 16, lambda ti=ti: sy.dma_start(
                q_t[:, :], q_tok_d.ap()[ti * 128:(ti + 1) * 128, :]))
            p.q(sy, 16, lambda ti=ti: sy.dma_start(
                k_t[:, :], k_tok_d.ap()[ti * 128:(ti + 1) * 128, :]))
            p.q(sy, 16, lambda ti=ti: sy.dma_start(
                v_t[:, :], v_tok_d.ap()[ti * 128:(ti + 1) * 128, :]))
            for i in range(8):
                for j in range(16):
                    cidx = i * 16 + j
                    p.q(v, 1, lambda i=i, j=j: v.tensor_mul(
                        scr[:, :], q_t[:, i * 64:(i + 1) * 64],
                        k_t[:, j * 64:(j + 1) * 64]))
                    p.q(v, 1, lambda cidx=cidx: v.reduce_sum(
                        att[:, cidx:cidx + 1], scr[:, :], axis=AX))
            for i in range(8):
                asl = slice(i * 16, (i + 1) * 16)
                p.q(v, 1, lambda asl=asl: v.reduce_max(m1[:, :], att[:, asl],
                                                       axis=AX))
                p.q(v, 1, lambda: v.tensor_scalar(nm1[:, :], m1[:, :], -1.0,
                                                  None, ALU.mult))
                p.q(s, 1, lambda asl=asl: s.activation(pa[:, :], att[:, asl],
                                                       AF.Exp, bias=nm1[:, 0:1]))
                p.q(v, 1, lambda: v.reduce_sum(ssum[:, :], pa[:, :], axis=AX))
                p.q(v, 1, lambda: v.reciprocal(ssum[:, :], ssum[:, :]))
                p.q(v, 1, lambda: v.tensor_scalar(pa[:, :], pa[:, :],
                                                  ssum[:, 0:1], None, ALU.mult))
                for j in range(16):
                    if j == 0:
                        p.q(v, 1, lambda j=j: v.tensor_scalar(
                            acc64[:, :], v_t[:, j * 64:(j + 1) * 64],
                            pa[:, j:j + 1], None, ALU.mult))
                    else:
                        p.q(v, 1, lambda j=j: v.scalar_tensor_tensor(
                            acc64[:, :], v_t[:, j * 64:(j + 1) * 64],
                            pa[:, j:j + 1], acc64[:, :], ALU.mult, ALU.add))
                p.q(v, 1, lambda: v.tensor_copy(o64[:, :], acc64[:, :]))
                p.q(sy, 16, lambda i=i, ti=ti: sy.dma_start(
                    outh_d.ap()[i, ti * 128:(ti + 1) * 128, :], o64[:, :]))

    with p.scoped() as sc:
        orr = sc.sb("l1_or", [128, D], BF16)
        orT = sc.sb("l1_orT", [128, D], BF16)
        ptr = sc.ps("l1_orps", [128, 128], BF16)
        ohv = outh_d.ap().rearrange("i (gg sixteen) d -> i gg (sixteen d)",
                                   sixteen=16)
        for ti2 in range(TQ // 128):
            i_blk = ti2 // 2
            g0 = (ti2 % 2) * 128
            p.q(sy, 16, lambda i_blk=i_blk, g0=g0: sy.dma_start(
                orr[:, :], ohv[i_blk, g0:g0 + 128, :]))
            for dc in range(8):
                p.q(t, 1, lambda dc=dc: t.transpose(
                    ptr[:, :], orr[:, dc * 128:(dc + 1) * 128], c["ident"][:, :]))
                p.q(v, 1, lambda dc=dc: v.tensor_copy(
                    orT[:, dc * 128:(dc + 1) * 128], ptr[:, :]))
            p.q(sy, 16, lambda ti2=ti2: sy.dma_start(
                orT_d.ap()[:, :, ti2 * 128:(ti2 + 1) * 128],
                orT.rearrange("p (dc n) -> p dc n", dc=8)))

    with p.scoped() as sc:
        wo_sb = load_weight_ws(p, sc, woT_d.ap(), D, D)
        bo_rep = emit_bias_replicate(p, c, sc, bo_d.ap())
        emit_final_proj_residual(p, c, wo_sb, orT_d, xo_d, y_d, bias_rep=bo_rep)
    p.finish()
    return p



def build_l2_attn():
    p = Prog()
    nc = p.nc
    v, s, sy, t, g = nc.vector, nc.scalar, nc.sync, nc.tensor, nc.gpsimd
    x_d = nc.dram_tensor("x_full", [SEQ, D], F32, kind="ExternalInput").ap()
    xo_d = nc.dram_tensor("x_own", [TQ, D], F32, kind="ExternalInput").ap()
    wqT_d = nc.dram_tensor("wqT", [D, D], F32, kind="ExternalInput")
    wkT_d = nc.dram_tensor("wkT", [D, D], F32, kind="ExternalInput")
    wvT_d = nc.dram_tensor("wvT", [D, D], F32, kind="ExternalInput")
    woT_d = nc.dram_tensor("woT", [D, D], F32, kind="ExternalInput")
    bq_d = nc.dram_tensor("bq_cols", [128, 8], F32, kind="ExternalInput")
    bk_d = nc.dram_tensor("bk_cols", [128, 8], F32, kind="ExternalInput")
    bv_d = nc.dram_tensor("bv_cols", [128, 8], F32, kind="ExternalInput")
    bo_d = nc.dram_tensor("bo", [D], F32, kind="ExternalInput")
    EeT_d = nc.dram_tensor("EextT", [DH, 6144], F32, kind="ExternalInput")
    y_d = nc.dram_tensor("y", [TQ, D], F32, kind="ExternalOutput").ap()

    xhatT_d = p.dram("xhatT", [128, 8, SEQ], BF16)
    xhatTq_d = p.dram("xhatTq", [128, 8, TQ], BF16)
    kT_d = p.dram("kT", [128, 8, SEQ], BF16)
    vT_d = p.dram("vT", [128, 8, SEQ], BF16)
    qT_d = p.dram("qT", [128, 8, TQ], BF16)
    mhneg_d = p.dram("mhneg", [NH, TQ], F32)
    ctx_tok_d = p.dram("ctx_tok", [TQ, D], BF16)
    ctxT_d = p.dram("ctxT", [128, 8, TQ], BF16)
    RS = 4352
    scr_d = p.dram("skew", [1, 128 * RS + 8192], BF16)

    c = setup_consts(p)
    emit_ln_transpose(p, c, x_d, 0, SEQ, xhatT_d)
    emit_ln_transpose(p, c, xo_d, 0, TQ, xhatTq_d)
    qsc = 1.0 / math.sqrt(DH)
    with p.scoped() as sc:
        bkc = load_bias_cols(p, sc, bk_d.ap(), D)
        wk_sb = load_weight_ws(p, sc, wkT_d.ap(), D, D)
        emit_proj_ws(p, wk_sb, D, D, xhatT_d, SEQ, kT_d, bias_cols=bkc,
                     scale=qsc)
    with p.scoped() as sc:
        bvc = load_bias_cols(p, sc, bv_d.ap(), D)
        wv_sb = load_weight_ws(p, sc, wvT_d.ap(), D, D)
        emit_proj_ws(p, wv_sb, D, D, xhatT_d, SEQ, vT_d, bias_cols=bvc)
    with p.scoped() as sc:
        bqc = load_bias_cols(p, sc, bq_d.ap(), D)
        wq_sb = load_weight_ws(p, sc, wqT_d.ap(), D, D)
        emit_proj_ws(p, wq_sb, D, D, xhatTq_d, TQ, qT_d, bias_cols=bqc)

    kmax2 = p.sb("l2_kmax2", [1, 1], F32)
    emax2 = p.sb("l2_emax2", [1, 1], F32)
    bscale = p.sb("l2_bsc", [1, 1], F32)
    emit_norm2_max(p, c, kT_d, range(8), SEQ, kmax2)
    E2 = p.sb("l2_E2", [128, 6144], BF16)
    p.q(g, 16, lambda: g.dma_start(E2[0:64, :], EeT_d.ap()))
    p.q(g, 16, lambda: g.dma_start(E2[64:128, :], EeT_d.ap()))
    with p.scoped() as sc:
        esq = sc.sb("l2_esq", [64, 512], BF16)
        t11 = sc.sb("l2_t11", [1, 1], F32)
        pn = sc.ps("l2_pn", [1, 512], F32)
        p.q(v, 1, lambda: v.memset(emax2[:, :], 0.0))
        for nn in range(12):
            p.q(s, 1, lambda nn=nn: s.activation(
                esq[:, :], E2[0:64, nn * 512:(nn + 1) * 512], AF.Square))
            p.q(t, 1, lambda: t.matmul(pn[:, :], c["ones_col_bf"][0:64, 0:1],
                                       esq[0:64, :], start=True, stop=True))
            p.q(v, 1, lambda: v.reduce_max(t11[:, :], pn[0:1, :], axis=AX))
            p.q(v, 1, lambda: v.tensor_tensor(emax2[:, :], emax2[:, :],
                                              t11[:, :], ALU.max))
    p.q(s, 1, lambda: s.activation(bscale[:, :], kmax2[:, :], AF.Sqrt))
    p.q(s, 1, lambda: s.activation(emax2[:, :], emax2[:, :], AF.Sqrt))
    p.q(v, 1, lambda: v.tensor_add(bscale[:, :], bscale[:, :], emax2[:, :]))

    with p.scoped() as sc:
        qin = sc.sb("l2_qin", [128, 512], BF16)
        qsq = sc.sb("l2_qsq", [128, 512], BF16)
        qrow = sc.sb("l2_qrow", [1, 512], F32)
        pn = sc.ps("l2_pn2", [1, 512], F32)
        for pr in range(8):
            for n in range(TQ // 512):
                p.q(sy, 16, lambda pr=pr, n=n: sy.dma_start(
                    qin[:, :], qT_d.ap()[:, pr, n * 512:(n + 1) * 512]))
                p.q(s, 1, lambda: s.activation(qsq[:, :], qin[:, :], AF.Square))
                for hh in (0, 1):
                    p.q(t, 1, lambda hh=hh: t.matmul(
                        pn[:, :], c["ones_col_bf"][hh * 64:hh * 64 + 64, 0:1],
                        qsq[hh * 64:hh * 64 + 64, :], start=True, stop=True))
                    p.q(s, 1, lambda: s.activation(qrow[:, :], pn[0:1, :],
                                                   AF.Sqrt))
                    p.q(v, 1, lambda: v.tensor_scalar(
                        qrow[:, :], qrow[:, :], bscale[0:1, 0:1], None,
                        ALU.mult))
                    p.q(v, 1, lambda: v.tensor_scalar(
                        qrow[:, :], qrow[:, :], -1.0, None, ALU.mult))
                    p.q(sy, 16, lambda pr=pr, hh=hh, n=n: sy.dma_start(
                        mhneg_d.ap()[2 * pr + hh, n * 512:(n + 1) * 512][None, :],
                        qrow[:, :]))

    with p.scoped() as sc:
        qpair = sc.sb("l2_qp", [128, TQ], BF16)
        kpair = sc.sb("l2_kp", [128, SEQ], BF16)
        vpair = sc.sb("l2_vp", [128, SEQ], BF16)
        Vtok = sc.sb("l2_vt", [128, 32 * DH], BF16)
        SE = sc.sb("l2_SE", [128, 4224], BF16)
        relb = sc.sb("l2_rel", [128, SEQ], BF16)
        Pq = sc.sb("l2_P", [128, SEQ], BF16)
        PT = sc.sb("l2_PT", [128, SEQ], BF16)
        mhc = sc.sb("l2_mhc", [128, 1], F32)
        dpart = sc.sb("l2_dp", [128, 8], F32)
        dsum = sc.sb("l2_ds", [128, 1], F32)
        ctxb = sc.sb("l2_ctxb", [128, DH], BF16)
        ps_se = sc.ps("l2_psse", [128, 512], F32)
        ps_s = sc.ps("l2_pss", [128, 512], F32)
        ps_vt = sc.ps("l2_psvt", [128, DH], BF16)
        ps_ctx = sc.ps("l2_psctx", [128, DH], F32)
        ps_pt = sc.ps("l2_pspt", [128, 128], BF16)
        for h in range(NH):
            pr, hh = h // 2, h % 2
            if hh == 0:
                p.q(sy, 16, lambda pr=pr: sy.dma_start(
                    qpair[:, :], qT_d.ap()[:, pr, :]))
                p.q(sy, 16, lambda pr=pr: sy.dma_start(
                    kpair[:, :], kT_d.ap()[:, pr, :]))
                p.q(sy, 16, lambda pr=pr: sy.dma_start(
                    vpair[:, :], vT_d.ap()[:, pr, :]))
            for kt in range(32):
                p.q(t, 1, lambda kt=kt, hh=hh: t.transpose(
                    ps_vt[:, :], vpair[hh * 64:hh * 64 + 64,
                                       kt * 128:(kt + 1) * 128],
                    c["ident"][hh * 64:hh * 64 + 64, hh * 64:hh * 64 + 64]))
                p.q(v, 1, lambda kt=kt: v.tensor_copy(
                    Vtok[:, kt * DH:(kt + 1) * DH], ps_vt[:, :]))
            for qt in range(TQ // 128):
                p.q(sy, 16, lambda h=h, qt=qt: sy.dma_start(
                    mhc[:, :],
                    mhneg_d.ap()[h, qt * 128:(qt + 1) * 128][:, None]))
                a_loc = 1920 - qt * 128
                for j in range(9):
                    w = 512 if j < 8 else 128
                    p.q(t, 1, lambda j=j, w=w, hh=hh, qt=qt, a_loc=a_loc:
                        t.matmul(
                            ps_se[:, 0:w],
                            qpair[hh * 64:hh * 64 + 64, qt * 128:(qt + 1) * 128],
                            E2[hh * 64:hh * 64 + 64,
                               a_loc + j * 512:a_loc + j * 512 + w],
                            start=True, stop=True))
                    p.q(v, 1, lambda j=j, w=w: v.tensor_copy(
                        SE[:, j * 512:j * 512 + w], ps_se[:, 0:w]))
                p.q(sy, 16, lambda: sy.dma_start(
                    bass.AP(scr_d, 0, [[RS + 1, 128], [1, 4224]]), SE[:, :]))
                p.q(sy, 16, lambda: sy.dma_start(
                    relb[:, :], bass.AP(scr_d, 127, [[RS, 128], [1, SEQ]])))
                for kb in range(8):
                    p.q(t, 0, lambda kb=kb, hh=hh, qt=qt: t.matmul(
                        ps_s[:, :],
                        qpair[hh * 64:hh * 64 + 64, qt * 128:(qt + 1) * 128],
                        kpair[hh * 64:hh * 64 + 64, kb * 512:(kb + 1) * 512],
                        start=True, stop=False))
                    p.q(t, 1, lambda kb=kb: t.matmul(
                        ps_s[:, :], c["ident"][:, :],
                        relb[:, kb * 512:(kb + 1) * 512],
                        start=False, stop=True, skip_group_check=True),
                        wait=False)
                    p.q(s, 1, lambda kb=kb: s.activation(
                        Pq[:, kb * 512:(kb + 1) * 512], ps_s[:, :], AF.Exp,
                        bias=mhc[:, 0:1], accum_out=dpart[:, kb:kb + 1]))
                p.q(v, 1, lambda: v.reduce_sum(dsum[:, :], dpart[:, :], axis=AX))
                p.q(v, 1, lambda: v.reciprocal(dsum[:, :], dsum[:, :]))
                for kt in range(32):
                    p.q(t, 1, lambda kt=kt: t.transpose(
                        ps_pt[:, :], Pq[:, kt * 128:(kt + 1) * 128],
                        c["ident"][:, :]))
                    p.q(v, 1, lambda kt=kt: v.tensor_copy(
                        PT[:, kt * 128:(kt + 1) * 128], ps_pt[:, :]))
                for kt in range(32):
                    p.q(t, 1 if kt == 31 else 0, lambda kt=kt: t.matmul(
                        ps_ctx[:, :], PT[:, kt * 128:(kt + 1) * 128],
                        Vtok[:, kt * DH:(kt + 1) * DH],
                        start=(kt == 0), stop=(kt == 31)), wait=(kt == 0))
                p.q(v, 1, lambda: v.tensor_scalar(
                    ctxb[:, :], ps_ctx[:, :], dsum[:, 0:1], None, ALU.mult))
                p.q(sy, 16, lambda qt=qt, h=h: sy.dma_start(
                    ctx_tok_d.ap()[qt * 128:(qt + 1) * 128,
                                   h * DH:(h + 1) * DH], ctxb[:, :]))

    emit_tile_transpose_pass(p, c, ctx_tok_d.ap(), TQ, ctxT_d)
    with p.scoped() as sc:
        wo_sb = load_weight_ws(p, sc, woT_d.ap(), D, D)
        bo_rep = emit_bias_replicate(p, c, sc, bo_d.ap())
        emit_final_proj_residual(p, c, wo_sb, ctxT_d, xo_d, y_d, bias_rep=bo_rep)
    p.finish()
    return p



def build_l3_attn():
    p = Prog()
    nc = p.nc
    v, s, sy, t, g = nc.vector, nc.scalar, nc.sync, nc.tensor, nc.gpsimd
    x_d = nc.dram_tensor("x_full", [SEQ, D], F32, kind="ExternalInput").ap()
    xo_d = nc.dram_tensor("x_own", [TQ, D], F32, kind="ExternalInput").ap()
    g1_d = nc.dram_tensor("ln1g", [D], F32, kind="ExternalInput")
    b1_d = nc.dram_tensor("ln1b", [D], F32, kind="ExternalInput")
    g2_d = nc.dram_tensor("ln2g", [D], F32, kind="ExternalInput")
    b2_d = nc.dram_tensor("ln2b", [D], F32, kind="ExternalInput")
    qU_d = nc.dram_tensor("qU", [D, NH * 48], F32, kind="ExternalInput")
    kU_d = nc.dram_tensor("kU", [D, NH * 48], F32, kind="ExternalInput")
    vU_d = nc.dram_tensor("vU", [D, NH * 48], F32, kind="ExternalInput")
    qV_d = nc.dram_tensor("qV", [48, NH * DH], F32, kind="ExternalInput")
    kV_d = nc.dram_tensor("kV", [48, NH * DH], F32, kind="ExternalInput")
    vV_d = nc.dram_tensor("vV", [48, NH * DH], F32, kind="ExternalInput")
    qb_d = nc.dram_tensor("qb_colsT", [DH, NH], F32, kind="ExternalInput")
    kb_d = nc.dram_tensor("kb_colsT", [DH, NH], F32, kind="ExternalInput")
    vb_d = nc.dram_tensor("vb_colsT", [DH, NH], F32, kind="ExternalInput")
    qb_row_d = nc.dram_tensor("qb_row", [1, D], F32, kind="ExternalInput")
    kb_row_d = nc.dram_tensor("kb_row", [1, D], F32, kind="ExternalInput")
    oU_d = nc.dram_tensor("outU", [D, 768], F32, kind="ExternalInput")
    oV_d = nc.dram_tensor("outV", [768, D], F32, kind="ExternalInput")
    ob_d = nc.dram_tensor("outb", [D], F32, kind="ExternalInput")
    f1U_d = nc.dram_tensor("fc1U", [D, 512], F32, kind="ExternalInput")
    f1V_d = nc.dram_tensor("fc1V", [512, 4096], F32, kind="ExternalInput")
    f1b_d = nc.dram_tensor("fc1b_cols", [128, 32], F32, kind="ExternalInput")
    f2U_d = nc.dram_tensor("fc2U", [4096, 512], F32, kind="ExternalInput")
    f2V_d = nc.dram_tensor("fc2V", [512, D], F32, kind="ExternalInput")
    f2b_d = nc.dram_tensor("fc2b", [D], F32, kind="ExternalInput")
    posq_d = nc.dram_tensor("posq_row", [1, TQ], F32, kind="ExternalInput")
    posk_d = nc.dram_tensor("posk_cols", [128, 32], F32, kind="ExternalInput")
    y_d = nc.dram_tensor("y", [TQ, D], F32, kind="ExternalOutput").ap()

    aT_d = p.dram("aT", [128, 8, SEQ], BF16)
    aTq_d = p.dram("aTq", [128, 8, TQ], BF16)
    t_own_d = p.dram("t_own", [TQ, D], F32)
    pq1_d = p.dram("pq1", [128, 8, TQ], BF16)
    pk1_d = p.dram("pk1", [128, 8, SEQ], BF16)
    pv1_d = p.dram("pv1", [128, 8, SEQ], BF16)
    ctxT_d = p.dram("ctxT", [128, 8, TQ], BF16)
    h_d = p.dram("hmid", [TQ, D], F32)
    mT_d = p.dram("mT", [128, 8, TQ], BF16)
    f1T_d = p.dram("f1T", [128, 4, TQ], BF16)
    fmT_d = p.dram("fmT", [128, 32, TQ], BF16)
    f2T_d = p.dram("f2T", [128, 4, TQ], BF16)
    oT_d = p.dram("oT", [128, 6, TQ], BF16)

    c = setup_consts(p)
    g1rep = emit_bias_replicate(p, c, Scope(p, p.ctx), g1_d.ap())
    b1rep = emit_bias_replicate(p, c, Scope(p, p.ctx), b1_d.ap())
    emit_ln_transpose(p, c, x_d, 0, SEQ, aT_d, gb=(g1rep, b1rep))
    emit_ln_transpose(p, c, xo_d, 0, TQ, aTq_d, gb=(g1rep, b1rep),
                      f32_d=t_own_d.ap())

    def stage1(u_d, srcT_d, T, out_d):
        with p.scoped() as sc:
            dense = sc.sb(p.uid("s1d"), [128, 8 * NH * 48], BF16)
            wpad = sc.sb(p.uid("s1w"), [128, 8 * 1024], BF16)
            p.q(g, 16, lambda: g.dma_start(
                dense.rearrange("p (dc hr) -> p dc hr", dc=8),
                u_d.ap().rearrange("(dc p) hr -> p dc hr", p=128)))
            p.q(g, 1, lambda: g.memset(wpad[:, :], 0.0))
            for dc in range(8):
                for h in range(NH):
                    src_off = dc * (NH * 48) + h * 48
                    dst_off = dc * 1024 + (h // 2) * 128 + (h % 2) * 64
                    p.q(v, 1, lambda src_off=src_off, dst_off=dst_off:
                        v.tensor_copy(wpad[:, dst_off:dst_off + 48],
                                      dense[:, src_off:src_off + 48]))
            emit_proj_ws(p, wpad, D, 1024, srcT_d, T, out_d, w_dout=1024)

    stage1(kU_d, aT_d, SEQ, pk1_d)
    stage1(vU_d, aT_d, SEQ, pv1_d)
    stage1(qU_d, aTq_d, TQ, pq1_d)

    p1qm = p.sb("l3_p1qm", [1, 1], F32)
    p1km = p.sb("l3_p1km", [1, 1], F32)
    emit_norm2_max(p, c, pq1_d, range(8), TQ, p1qm)
    emit_norm2_max(p, c, pk1_d, range(8), SEQ, p1km)

    def vfmax(v_d, out11):
        with p.scoped() as sc:
            vt = sc.sb(p.uid("vf"), [48, NH * DH], BF16)
            vsq = sc.sb(p.uid("vfsq"), [48, NH * DH], BF16)
            row = sc.sb(p.uid("vfr"), [1, NH * DH], F32)
            t11 = sc.sb(p.uid("vf11"), [1, 1], F32)
            pn = sc.ps(p.uid("vfp"), [1, 512], F32)
            p.q(g, 16, lambda: g.dma_start(vt[:, :], v_d.ap()))
            p.q(s, 1, lambda: s.activation(vsq[:, :], vt[:, :], AF.Square))
            for jj in range(NH * DH // 512):
                p.q(t, 1, lambda jj=jj: t.matmul(
                    pn[:, :], c["ones_col_bf"][0:48, 0:1],
                    vsq[0:48, jj * 512:(jj + 1) * 512], start=True, stop=True))
                p.q(v, 1, lambda jj=jj: v.tensor_copy(
                    row[0:1, jj * 512:(jj + 1) * 512], pn[0:1, :]))
            p.q(v, 1, lambda: v.memset(out11[:, :], 0.0))
            for h in range(NH):
                p.q(v, 1, lambda h=h: v.reduce_sum(
                    t11[:, :], row[0:1, h * DH:(h + 1) * DH], axis=AX))
                p.q(v, 1, lambda: v.tensor_tensor(out11[:, :], out11[:, :],
                                                  t11[:, :], ALU.max))

    vfq = p.sb("l3_vfq", [1, 1], F32)
    vfk = p.sb("l3_vfk", [1, 1], F32)
    vfmax(qV_d, vfq)
    vfmax(kV_d, vfk)

    def rowmax_abs(row_d, out11):
        with p.scoped() as sc:
            r = sc.sb(p.uid("rm"), [1, D], F32)
            p.q(sy, 16, lambda: sy.dma_start(r[:, :], row_d.ap()))
            p.q(v, 1, lambda: v.reduce_max(out11[:, :], r[:, :], axis=AX,
                                           apply_absolute_value=True))

    qbm = p.sb("l3_qbm", [1, 1], F32)
    kbm = p.sb("l3_kbm", [1, 1], F32)
    rowmax_abs(qb_row_d, qbm)
    rowmax_abs(kb_row_d, kbm)
    bq11 = p.sb("l3_bq", [1, 1], F32)
    bk11 = p.sb("l3_bk", [1, 1], F32)
    mh_row = p.sb("l3_mhrow", [1, TQ], F32)
    for (pm, vf, bm, out, scale) in ((p1qm, vfq, qbm, bq11, 1.0 / 8.0),
                                     (p1km, vfk, kbm, bk11, 1.0)):
        p.q(s, 1, lambda pm=pm, out=out: s.activation(out[:, :], pm[:, :],
                                                      AF.Sqrt))
        p.q(s, 1, lambda vf=vf: s.activation(vf[:, :], vf[:, :], AF.Sqrt))
        p.q(v, 1, lambda out=out, vf=vf: v.tensor_mul(out[:, :], out[:, :],
                                                      vf[:, :]))
        p.q(v, 1, lambda out=out, bm=bm: v.tensor_add(out[:, :], out[:, :],
                                                      bm[:, :]))
        p.q(v, 1, lambda out=out, scale=scale: v.tensor_scalar(
            out[:, :], out[:, :], scale, None, ALU.mult))
    emit_mhat_row(p, c, bq11, bk11, mh_row, TQ, take_sqrt=False)

    mask_q = p.sb("l3_mq", [128, 4 * 512], F32)
    kpos = p.sb("l3_kp", [128, 32], F32)
    posqr = p.sb("l3_pq", [1, TQ], F32)
    mtile = p.sb("l3_mt", [128, 512], BF16)
    p.q(sy, 16, lambda: sy.dma_start(posqr[:, :], posq_d.ap()))
    p.q(sy, 16, lambda: sy.dma_start(kpos[:, :], posk_d.ap()))
    emit_row_replicate(p, c, posqr, mask_q, TQ, nrows=128)

    s2q = p.sb("l3_s2q", [64, DH], BF16)
    s2k = p.sb("l3_s2k", [64, DH], BF16)
    s2v = p.sb("l3_s2v", [64, DH], BF16)
    p1buf = p.sb("l3_p1b", [64, SEQ], BF16)
    qbc = p.sb("l3_qbc", [DH, NH], F32)
    kbc = p.sb("l3_kbc", [DH, NH], F32)
    vbc = p.sb("l3_vbc", [DH, NH], F32)
    p.q(sy, 16, lambda: sy.dma_start(qbc[:, :], qb_d.ap()))
    p.q(sy, 16, lambda: sy.dma_start(kbc[:, :], kb_d.ap()))
    p.q(sy, 16, lambda: sy.dma_start(vbc[:, :], vb_d.ap()))
    for buf in (s2q, s2k, s2v):
        p.q(g, 1, lambda buf=buf: g.memset(buf[:, :], 0.0))

    def kv_stage2(h, kq, qq, vt_sb):
        pr, hh = h // 2, h % 2
        for (s2, w_d) in ((s2q, qV_d), (s2k, kV_d), (s2v, vV_d)):
            p.q(g, 16, lambda s2=s2, w_d=w_d, h=h: g.dma_start(
                s2[0:48, :], w_d.ap()[:, h * DH:(h + 1) * DH]))
        with p.scoped() as sc:
            pp = sc.ps(p.uid("s2p"), [DH, 512], F32)
            for (src_d, T_, dst, s2w, bcol, scale) in (
                    (pk1_d, SEQ, kq, s2k, kbc, None),
                    (pq1_d, TQ, qq, s2q, qbc, 1.0 / 8.0),
                    (pv1_d, SEQ, vt_sb, s2v, vbc, None)):
                p.q(sy, 16, lambda src_d=src_d, T_=T_, hh=hh, pr=pr:
                    sy.dma_start(p1buf[0:64, 0:T_],
                                 src_d.ap()[hh * 64:hh * 64 + 64, pr, :]))
                for n in range(T_ // 512):
                    p.q(t, 1, lambda s2w=s2w, n=n:
                        t.matmul(pp[:, :], s2w[0:64, :],
                                 p1buf[0:64, n * 512:(n + 1) * 512],
                                 start=True, stop=True))
                    if scale is None:
                        p.q(v, 1, lambda dst=dst, n=n, bcol=bcol, h=h:
                            v.tensor_scalar(dst[0:DH, n * 512:(n + 1) * 512],
                                            pp[:, :], bcol[:, h:h + 1], None,
                                            ALU.add))
                    else:
                        p.q(v, 1, lambda dst=dst, n=n, bcol=bcol, h=h, scale=scale:
                            v.tensor_scalar(dst[0:DH, n * 512:(n + 1) * 512],
                                            pp[:, :], bcol[:, h:h + 1], scale,
                                            ALU.add, ALU.mult))

    def maskmul(PT_slice, kt, qg):
        p.q(v, 1, lambda: v.tensor_scalar(
            mtile[:, :], mask_q[:, qg * 512:(qg + 1) * 512],
            kpos[:, kt:kt + 1], None, ALU.is_ge))
        p.q(v, 1, lambda PT_slice=PT_slice: v.tensor_mul(
            PT_slice, PT_slice, mtile[:, :]))

    emit_attn_core(p, c, None, None, None, ctxT_d, mh_row, kv_stage2=kv_stage2,
                   maskmul=maskmul)

    with p.scoped() as sc:
        oU_sb = load_weight_ws(p, sc, oU_d.ap(), D, 768)
        emit_proj_ws(p, oU_sb, D, 768, ctxT_d, TQ, oT_d, w_dout=768)
    with p.scoped() as sc:
        oV_sb = load_weight_ws(p, sc, oV_d.ap(), 768, D)
        ob_rep = emit_bias_replicate(p, c, sc, ob_d.ap())
        xres = sc.sb("l3_xres", [128, D], F32)
        hout = sc.sb("l3_hout", [128, D], F32)

        def evac_h(ti, og, pp):
            if og == 0:
                p.q(sy, 16, lambda ti=ti: sy.dma_start(
                    xres[:, :], t_own_d.ap()[ti * 128:(ti + 1) * 128, :]))
            osl = slice(og * 512, (og + 1) * 512)
            p.q(v, 1, lambda osl=osl, pp=pp: v.tensor_add(
                hout[:, osl], pp[:, :], ob_rep[:, osl]))
            p.q(v, 1, lambda osl=osl: v.tensor_add(
                hout[:, osl], hout[:, osl], xres[:, osl]))
            if og == 1:
                p.q(sy, 16, lambda ti=ti: sy.dma_start(
                    h_d.ap()[ti * 128:(ti + 1) * 128, :], hout[:, :]))

        emit_proj_xs(p, oV_sb, 768, D, oT_d, TQ, evac_h)

    g2rep = emit_bias_replicate(p, c, Scope(p, p.ctx), g2_d.ap())
    b2rep = emit_bias_replicate(p, c, Scope(p, p.ctx), b2_d.ap())
    emit_ln_transpose(p, c, h_d.ap(), 0, TQ, mT_d, gb=(g2rep, b2rep))
    with p.scoped() as sc:
        f1U_sb = load_weight_ws(p, sc, f1U_d.ap(), D, 512)
        emit_proj_ws(p, f1U_sb, D, 512, mT_d, TQ, f1T_d, w_dout=512)
    with p.scoped() as sc:
        f1V_sb = load_weight_ws(p, sc, f1V_d.ap(), 512, 4096)
        f1b_sb = load_bias_cols(p, sc, f1b_d.ap(), 4096)
        emit_proj_ws(p, f1V_sb, 512, 4096, f1T_d, TQ, fmT_d, w_dout=4096,
                     bias_cols=f1b_sb, act=AF.Gelu_apprx_tanh)
    with p.scoped() as sc:
        f2U_sb = load_weight_ws(p, sc, f2U_d.ap(), 4096, 512)
        emit_proj_ws(p, f2U_sb, 4096, 512, fmT_d, TQ, f2T_d, w_dout=512)
    with p.scoped() as sc:
        f2V_sb = load_weight_ws(p, sc, f2V_d.ap(), 512, D)
        f2b_rep = emit_bias_replicate(p, c, sc, f2b_d.ap())
        emit_final_proj_residual(p, c, f2V_sb, f2T_d, xo_d, y_d,
                                 bias_rep=f2b_rep, extra_res_d=h_d.ap(),
                                 din=512)
    p.finish()
    return p



_POSK = np.arange(SEQ, dtype=np.float32).reshape(1, SEQ)


def _x_maps(x):
    maps = []
    for cid in range(NCORES):
        b, half = cid // 2, cid % 2
        maps.append({
            "x_full": np.ascontiguousarray(x[b]),
            "x_own": np.ascontiguousarray(x[b, half * TQ:(half + 1) * TQ]),
            "posq": np.ascontiguousarray(_POSK[:, half * TQ:(half + 1) * TQ]),
        })
    return maps


def _run(p, in_map_list):
    res = run_bass_kernel_spmd(p.nc, in_map_list,
                               core_ids=list(range(len(in_map_list))))
    p.close()
    return res.results


def _stitch_halves(results, key="y"):
    out = np.empty((4, SEQ, D), np.float32)
    for cid in range(NCORES):
        out[cid // 2, (cid % 2) * TQ:(cid % 2) * TQ + TQ, :] = results[cid][key]
    return out


def run_l0_attn(x, inp):
    p = build_attn_std("L0")
    wq = np.ascontiguousarray(inp["L0_attn_w_qkv"], dtype=np.float32)
    bq = hBC(inp["L0_attn_b_qkv"])
    wo = np.ascontiguousarray(inp["L0_attn_w_out"], dtype=np.float32)
    bo = np.ascontiguousarray(inp["L0_attn_b_out"], dtype=np.float32)
    frT = hT(np.asarray(inp["L0_attn_rope_freqs"], dtype=np.float32)[:, :, 0])
    maps = []
    for m in _x_maps(x):
        maps.append({
            "x_full": m["x_full"], "x_own": m["x_own"],
            "w_qkv": wq, "b_qkv": bq, "w_out": wo, "b_out": bo,
            "freqsT": frT, "posk_row": _POSK,
            "posq_row": m["posq"],
        })
    return _stitch_halves(_run(p, maps))


def run_l1_attn(x, inp):
    p = build_l1_attn()
    wk = hT(inp["L1_attn_Wk"])
    wv = hT(inp["L1_attn_Wv"])
    wo = hT(inp["L1_attn_Wo"])
    wqT = hT(inp["L1_attn_Wq"])
    bo = np.ascontiguousarray(inp["L1_attn_bo"], dtype=np.float32)
    maps = []
    for cid, m in enumerate(_x_maps(x)):
        half = cid % 2
        wq_own = np.ascontiguousarray(wqT[:, half * 512:(half + 1) * 512])
        maps.append({
            "x_full": m["x_full"], "x_own": m["x_own"],
            "wqT_own": wq_own, "wkT": wk, "wvT": wv, "woT": wo, "bo": bo,
        })
    return _stitch_halves(_run(p, maps))


def run_l0_ffn(x, inp):
    p = build_l0_ffn()
    wiT = hT(inp["L0_ffn_wi_w"])
    woT = hT(inp["L0_ffn_wo_w"])
    AsT = np.ascontiguousarray(
        np.asarray(inp["L0_ffn_lora_As"], np.float32).transpose(0, 2, 1))
    BsT = np.ascontiguousarray(
        np.asarray(inp["L0_ffn_lora_Bs"], np.float32).transpose(0, 2, 1))
    idx = np.ascontiguousarray(
        np.asarray(inp["L0_ffn_top_k_indices"], np.int32).reshape(1, 2))
    val = np.ascontiguousarray(
        np.asarray(inp["L0_ffn_top_k_values"], np.float32).reshape(1, 2))
    maps = []
    for m in _x_maps(x):
        maps.append({"x_own": m["x_own"], "wiT": wiT, "woT": woT,
                     "AsT": AsT, "BsT": BsT, "topk_idx": idx, "topk_val": val})
    return _stitch_halves(_run(p, maps))


def run_l1_ffn(x, inp):
    p = build_moe_ffn("L1")
    com = {
        "sgT": hT(inp["L1_ffn_sg_w"]), "suT": hT(inp["L1_ffn_su_w"]),
        "sdT": hT(inp["L1_ffn_sd_w"]), "rT": hT(inp["L1_ffn_router_w"]),
        "rb": np.ascontiguousarray(
            np.asarray(inp["L1_ffn_routing_bias"], np.float32).reshape(8, 1)),
        "wgT": np.ascontiguousarray(
            np.asarray(inp["L1_ffn_wg"], np.float32).transpose(0, 2, 1)),
        "wuT": np.ascontiguousarray(
            np.asarray(inp["L1_ffn_wu"], np.float32).transpose(0, 2, 1)),
        "wdT": np.ascontiguousarray(
            np.asarray(inp["L1_ffn_wd"], np.float32).transpose(0, 2, 1)),
    }
    maps = [dict(com, x_own=m["x_own"]) for m in _x_maps(x)]
    return _stitch_halves(_run(p, maps))


def run_l2_ffn(x, inp):
    p = build_moe_ffn("L2")
    bfc = np.asarray(inp["L2_ffn_bfc"], np.float32)
    com = {
        "WfsT": np.ascontiguousarray(np.asarray(inp["L2_ffn_Wfc_s"], np.float32)),
        "bfs_cols": hBC(inp["L2_ffn_bfc_s"]),
        "WpsT": np.ascontiguousarray(np.asarray(inp["L2_ffn_Wproj_s"], np.float32)),
        "bps": np.ascontiguousarray(np.asarray(inp["L2_ffn_bproj_s"], np.float32)),
        "WrT": np.ascontiguousarray(np.asarray(inp["L2_ffn_Wr"], np.float32)),
        "WfcT": np.ascontiguousarray(np.asarray(inp["L2_ffn_Wfc"], np.float32)),
        "bfc_cols": np.ascontiguousarray(
            np.stack([hBC(bfc[e]) for e in range(8)])),
        "WprT": np.ascontiguousarray(np.asarray(inp["L2_ffn_Wproj"], np.float32)),
        "bpr": np.ascontiguousarray(np.asarray(inp["L2_ffn_bproj"], np.float32)),
    }
    maps = [dict(com, x_own=m["x_own"]) for m in _x_maps(x)]
    return _stitch_halves(_run(p, maps))


def run_l3_ffn(x, inp):
    p = build_moe_ffn("L3")
    com = {
        "gateT": hT(inp["L3_ffn_gate_w"]),
        "w1T": np.ascontiguousarray(np.asarray(inp["L3_ffn_w1"], np.float32)),
        "w3T": np.ascontiguousarray(np.asarray(inp["L3_ffn_w3"], np.float32)),
        "w2T": np.ascontiguousarray(np.asarray(inp["L3_ffn_w2"], np.float32)),
    }
    maps = [dict(com, x_own=m["x_own"]) for m in _x_maps(x)]
    return _stitch_halves(_run(p, maps))


def _l2_eext(rel_emb):
    r = np.asarray(rel_emb, np.float32)
    idx = np.empty(8192, np.int64)
    cs = np.arange(8192)
    idx[:2048] = cs[:2048] + 2047
    idx[2048:6143] = cs[2048:6143] - 2048
    idx[6143:] = 4094
    return np.ascontiguousarray(r[idx].T)


def run_l2_attn(x, inp):
    p = build_l2_attn()
    com = {
        "wqT": hT(inp["L2_attn_Wq"]), "wkT": hT(inp["L2_attn_Wk"]),
        "wvT": hT(inp["L2_attn_Wv"]), "woT": hT(inp["L2_attn_Wo"]),
        "bq_cols": hBC(inp["L2_attn_bq"]), "bk_cols": hBC(inp["L2_attn_bk"]),
        "bv_cols": hBC(inp["L2_attn_bv"]),
        "bo": np.ascontiguousarray(np.asarray(inp["L2_attn_bo"], np.float32)),
    }
    Efull = _l2_eext(inp["L2_attn_rel_emb"])
    maps = []
    for cid, m in enumerate(_x_maps(x)):
        ho = (cid % 2) * TQ
        maps.append(dict(com, x_full=m["x_full"], x_own=m["x_own"],
                         EextT=np.ascontiguousarray(
                             Efull[:, 2048 - ho:8192 - ho])))
    return _stitch_halves(_run(p, maps))


def run_l3_attn(x, inp):
    p = build_l3_attn()
    qb = np.asarray(inp["L3_attn_q_b"], np.float32)
    kb = np.asarray(inp["L3_attn_k_b"], np.float32)
    vb = np.asarray(inp["L3_attn_v_b"], np.float32)
    com = {
        "ln1g": np.ascontiguousarray(np.asarray(inp["L3_attn_ln1_g"], np.float32)),
        "ln1b": np.ascontiguousarray(np.asarray(inp["L3_attn_ln1_b"], np.float32)),
        "ln2g": np.ascontiguousarray(np.asarray(inp["L3_attn_ln2_g"], np.float32)),
        "ln2b": np.ascontiguousarray(np.asarray(inp["L3_attn_ln2_b"], np.float32)),
        "qU": np.ascontiguousarray(
            np.asarray(inp["L3_attn_q_U"], np.float32).reshape(D, NH * 48)),
        "kU": np.ascontiguousarray(
            np.asarray(inp["L3_attn_k_U"], np.float32).reshape(D, NH * 48)),
        "vU": np.ascontiguousarray(
            np.asarray(inp["L3_attn_v_U"], np.float32).reshape(D, NH * 48)),
        "qV": np.ascontiguousarray(
            np.asarray(inp["L3_attn_q_V"], np.float32).transpose(1, 0, 2)
            .reshape(48, NH * DH)),
        "kV": np.ascontiguousarray(
            np.asarray(inp["L3_attn_k_V"], np.float32).transpose(1, 0, 2)
            .reshape(48, NH * DH)),
        "vV": np.ascontiguousarray(
            np.asarray(inp["L3_attn_v_V"], np.float32).transpose(1, 0, 2)
            .reshape(48, NH * DH)),
        "qb_colsT": hT(qb), "kb_colsT": hT(kb), "vb_colsT": hT(vb),
        "qb_row": np.ascontiguousarray(qb.reshape(1, D)),
        "kb_row": np.ascontiguousarray(kb.reshape(1, D)),
        "outU": np.ascontiguousarray(np.asarray(inp["L3_attn_out_U"], np.float32)),
        "outV": np.ascontiguousarray(np.asarray(inp["L3_attn_out_V"], np.float32)),
        "outb": np.ascontiguousarray(np.asarray(inp["L3_attn_out_b"], np.float32)),
        "fc1U": np.ascontiguousarray(np.asarray(inp["L3_attn_fc1_U"], np.float32)),
        "fc1V": np.ascontiguousarray(np.asarray(inp["L3_attn_fc1_V"], np.float32)),
        "fc1b_cols": hBC(inp["L3_attn_fc1_b"]),
        "fc2U": np.ascontiguousarray(np.asarray(inp["L3_attn_fc2_U"], np.float32)),
        "fc2V": np.ascontiguousarray(np.asarray(inp["L3_attn_fc2_V"], np.float32)),
        "fc2b": np.ascontiguousarray(np.asarray(inp["L3_attn_fc2_b"], np.float32)),
        "posk_cols": np.ascontiguousarray(
            np.arange(SEQ, dtype=np.float32).reshape(32, 128).T),
    }
    maps = []
    for cid, m in enumerate(_x_maps(x)):
        maps.append(dict(com, x_full=m["x_full"], x_own=m["x_own"],
                         posq_row=m["posq"]))
    return _stitch_halves(_run(p, maps))


def kernel(**inputs):
    x = np.asarray(inputs["L0_attn_x"], dtype=np.float32)
    x = run_l0_attn(x, inputs)
    x = run_l0_ffn(x, inputs)
    x = run_l1_attn(x, inputs)
    x = run_l1_ffn(x, inputs)
    x = run_l2_attn(x, inputs)
    x = run_l2_ffn(x, inputs)
    x = run_l3_attn(x, inputs)
    x = run_l3_ffn(x, inputs)
    return x
